# version 12
# baseline (speedup 1.0000x reference)
import math
from contextlib import ExitStack

import numpy as np
import ml_dtypes

import concourse.bass as bass
import concourse.mybir as mybir
from concourse.bass_utils import run_bass_kernel_spmd

F32 = mybir.dt.float32
BF16 = mybir.dt.bfloat16
I8 = mybir.dt.int8
AF = mybir.ActivationFunctionType
ALU = mybir.AluOpType

B, S, E, H, Dh = 2, 4096, 4096, 32, 128
FFI = 11008
SCALE = 1.0 / math.sqrt(128.0)
TOK = 1024
CTX = 5120
CTXT = 40
DT = 32
ITT = 86
EPS = 1e-5
NEG = -1.0e30

IC_SIZES = [15, 15, 14, 14, 14, 14]
IC_BASE = [0, 15, 30, 44, 58, 72]
DEBUG_DUMP = False
WCG = 16
DEG = 8


def _alibi_slopes(n: int) -> np.ndarray:
    cp2 = 2 ** int(math.floor(math.log2(n)))
    base = 2.0 ** (-(2.0 ** (-(math.log2(cp2) - 3))))
    slopes = base ** np.arange(1, cp2 + 1, dtype=np.float64)
    if cp2 != n:
        extra_base = 2.0 ** (-(2.0 ** (-(math.log2(2 * cp2) - 3))))
        rem = min(cp2, n - cp2)
        slopes = np.concatenate(
            [slopes, extra_base ** np.arange(1, 1 + 2 * rem, 2, dtype=np.float64)])
    return slopes.astype(np.float32)


SLOPES = _alibi_slopes(H)


def _lo(h, w):
    lo = 0
    for kt in range(36 + 4 * w):
        if SLOPES[h] * (128 * (kt + 1) - 1 - 4096 - 512 * w) < -150.0:
            lo = kt + 1
    return lo


LO = [[_lo(h, w) for w in range(2)] for h in range(H)]
for h in range(H):
    for w in range(2):
        assert LO[h][w] <= 32 + 4 * w


class Sem:
    def __init__(self, nc, st, name):
        self.h = st.enter_context(nc.semaphore(name))
        self.n = 0
        self.last_wait = {}

    def inc(self, instr, k=1):
        instr.then_inc(self.h, k)
        self.n += k
        return instr

    def wait(self, eng, val=None):
        v = self.n if val is None else val
        q = type(eng).__name__
        if self.last_wait.get(q, -1) >= v:
            return
        self.last_wait[q] = v
        eng.wait_ge(self.h, v)


def build_nc():
    nc = bass.Bass("TRN2", target_bir_lowering=False)

    x_ctx = nc.dram_tensor("x_ctx", [CTX, E], F32, kind="ExternalInput")
    w_qkv = nc.dram_tensor("w_qkv", [E, H * Dh + 2 * Dh], F32, kind="ExternalInput")
    w_c = nc.dram_tensor("w_c", [E, E], F32, kind="ExternalInput")
    gw8 = nc.dram_tensor("gw8", [E, FFI], I8, kind="ExternalInput")
    uw8 = nc.dram_tensor("uw8", [E, FFI], I8, kind="ExternalInput")
    dw8 = nc.dram_tensor("dw8", [FFI, E], I8, kind="ExternalInput")
    scal = nc.dram_tensor("scal", [128, 3], F32, kind="ExternalInput")
    aux_r2l = nc.dram_tensor("aux_r2l", [128, H * 128], BF16, kind="ExternalInput")
    aux_r2r = nc.dram_tensor("aux_r2r", [128, 512], BF16, kind="ExternalInput")
    aux_vones = nc.dram_tensor("aux_vones", [128, CTXT], F32, kind="ExternalInput")
    aux_bias = nc.dram_tensor("aux_bias", [128, H * 80], F32, kind="ExternalInput")
    aux_tri = nc.dram_tensor("aux_tri", [128, 128], F32, kind="ExternalInput")
    aux_id = nc.dram_tensor("aux_id", [128, 128], BF16, kind="ExternalInput")

    out = nc.dram_tensor("out", [TOK, E], F32, kind="ExternalOutput")
    if DEBUG_DUMP:
        dbg_xn = nc.dram_tensor("dbg_xn", [CTX, E], BF16, kind="ExternalOutput")
        dbg_at = nc.dram_tensor("dbg_at", [TOK, E], BF16, kind="ExternalOutput")
        dbg_k = nc.dram_tensor("dbg_k", [128, CTX], BF16, kind="ExternalOutput")
        dbg_m = nc.dram_tensor("dbg_m", [128, 15 * 1024], BF16, kind="ExternalOutput")

    xn_sp = nc.dram_tensor("xn_sp", [CTX, E], BF16, kind="Internal")
    attn_sp = nc.dram_tensor("attn_sp", [TOK, E], BF16, kind="Internal")
    gsc = nc.dram_tensor("gsc", [E, FFI], BF16, kind="Internal")
    usc = nc.dram_tensor("usc", [E, FFI], BF16, kind="Internal")
    dsc = nc.dram_tensor("dsc", [FFI, E], BF16, kind="Internal")

    st = ExitStack()
    e = st.enter_context

    sb_scal = e(nc.sbuf_tensor("sb_scal", [128, 3], F32))
    sb_usds = e(nc.sbuf_tensor("sb_usds", [128, 1], F32))
    sb_tri = e(nc.sbuf_tensor("sb_tri", [128, 128], F32))
    sb_id = e(nc.sbuf_tensor("sb_id", [128, 128], BF16))
    xn_own = e(nc.sbuf_tensor("xn_own", [128, DT * 1024], BF16))

    d_aux = Sem(nc, st, "d_aux")
    d_x = [Sem(nc, st, f"d_x{i}") for i in range(2)]
    d_xnw = [Sem(nc, st, f"d_xnw{i}") for i in range(10)]
    d_xt = [Sem(nc, st, f"d_xt{i}") for i in range(2)]
    d_wkv = Sem(nc, st, "d_wkv")
    d_gsc = Sem(nc, st, "d_gsc")
    d_wq = [Sem(nc, st, f"d_wq{i}") for i in range(2)]
    d_as = [Sem(nc, st, f"d_as{i}") for i in range(2)]
    d_at = Sem(nc, st, "d_at")
    d_wcq = [Sem(nc, st, f"d_wcq{i}") for i in range(2)]
    d_xo = [Sem(nc, st, f"d_xo{i}") for i in range(2)]
    d_out = [Sem(nc, st, f"d_out{i}") for i in range(8)]
    d_outa = [Sem(nc, st, f"d_outa{i}") for i in range(8)]
    d_xn2 = Sem(nc, st, "d_xn2")
    d_wg = [Sem(nc, st, f"d_wg{i}") for i in range(2)]
    d_dw = [Sem(nc, st, f"d_dw{i}") for i in range(2)]

    c_ln_v = Sem(nc, st, "c_ln_v")
    c_ln_s = Sem(nc, st, "c_ln_s")
    c_kv_t = Sem(nc, st, "c_kv_t")
    c_kv_e = Sem(nc, st, "c_kv_e")
    c_q = Sem(nc, st, "c_q")
    c_qe = Sem(nc, st, "c_qe")
    c_sc = Sem(nc, st, "c_sc")
    c_mk = Sem(nc, st, "c_mk")
    c_ex = Sem(nc, st, "c_ex")
    c_pv = Sem(nc, st, "c_pv")
    c_pve = Sem(nc, st, "c_pve")
    c_wc = Sem(nc, st, "c_wc")
    c_wce = Sem(nc, st, "c_wce")
    c_gu = Sem(nc, st, "c_gu")
    c_sl = Sem(nc, st, "c_sl")
    c_m = Sem(nc, st, "c_m")
    c_dn = Sem(nc, st, "c_dn")
    c_dne = Sem(nc, st, "c_dne")
    c_misc = Sem(nc, st, "c_misc")

    P01 = e(nc.psum_tensor("P01", [128, 1024], F32))
    P23 = e(nc.psum_tensor("P23", [128, 1024], F32))
    P45 = e(nc.psum_tensor("P45", [128, 1024], F32))
    P67 = e(nc.psum_tensor("P67", [128, 1024], F32))

    blk = st.enter_context(nc.Block())

    kv_scope = ExitStack()
    ke = kv_scope.enter_context
    K_sb = ke(nc.sbuf_tensor("K_sb", [128, CTX], BF16, side="right"))
    V_sb = ke(nc.sbuf_tensor("V_sb", [128, CTXT * 129], BF16, side="right"))
    vones_sb = ke(nc.sbuf_tensor("vones_sb", [128, CTXT], F32, side="right"))

    ln_scope = ExitStack()
    le = ln_scope.enter_context
    xbuf = [le(nc.sbuf_tensor(f"xbuf{i}", [128, E], F32, side="right")) for i in range(2)]
    xnbuf = [le(nc.sbuf_tensor("xnbuf0", [128, E], BF16, side="right"))]
    stt = le(nc.sbuf_tensor("stt", [128, 48], F32, side="right"))
    mv = le(nc.sbuf_tensor("mv", [128, 2], F32, side="right"))
    veps = le(nc.sbuf_tensor("veps", [128, 1], F32, side="right"))
    lnl = le(nc.sbuf_tensor("lnl", [128, 1], F32, side="right"))
    rstd = le(nc.sbuf_tensor("rstd", [128, 1], F32, side="right"))
    nbias = le(nc.sbuf_tensor("nbias", [128, 1], F32, side="right"))

    wkv_scope = ExitStack()
    wkv_sb = wkv_scope.enter_context(nc.sbuf_tensor("wkv_sb", [128, DT * 256], BF16, side="right"))
    xt_scope = ExitStack()
    xt_sl = [xt_scope.enter_context(
        nc.sbuf_tensor("xt_sl0", [128, DT * 512], BF16, side="right"))]

    cast_jobs = []
    for r in range(DT):
        cast_jobs.append((gsc.ap()[r * 128:(r + 1) * 128, :],
                          gw8.ap()[r * 128:(r + 1) * 128, :]))
    for r in range(DT):
        cast_jobs.append((usc.ap()[r * 128:(r + 1) * 128, :],
                          uw8.ap()[r * 128:(r + 1) * 128, :]))
    for r in range(ITT):
        cast_jobs.append((dsc.ap()[r * 128:(r + 1) * 128, :],
                          dw8.ap()[r * 128:(r + 1) * 128, :]))
    cast_pos = [0]

    def emit_casts(g, k):
        for _ in range(k):
            if cast_pos[0] < len(cast_jobs):
                dst, src = cast_jobs[cast_pos[0]]
                d_gsc.inc(g.dma_start(dst, src), 16)
                cast_pos[0] += 1

    att_scope = ExitStack()
    ae = att_scope.enter_context
    wq_sl = [ae(nc.sbuf_tensor(f"wq_sl{i}", [128, DT * 128], BF16, side="right")) for i in range(2)]
    q_sl = [ae(nc.sbuf_tensor(f"q_sl{i}", [128, 1024], BF16, side="right")) for i in range(2)]
    probs = [ae(nc.sbuf_tensor(f"probs{i}", [128, 512], BF16, side="right")) for i in range(4)]
    rec_sb = ae(nc.sbuf_tensor("rec_sb", [128, 4], F32, side="right"))
    astage = [ae(nc.sbuf_tensor(f"astage{i}", [128, 128], BF16, side="right")) for i in range(2)]
    r2l_sb = ae(nc.sbuf_tensor("r2l_sb", [128, H * 128], BF16, side="right"))
    r2r_sb = ae(nc.sbuf_tensor("r2r_sb", [128, 512], BF16, side="right"))
    bias_sb = ae(nc.sbuf_tensor("bias_sb", [128, 2 * 80], F32, side="right"))

    def em(s):
        d_aux.inc(s.dma_start(sb_tri[:], aux_tri.ap()), 16)
        d_aux.inc(s.dma_start(sb_id[:], aux_id.ap()), 16)
        d_aux.inc(s.dma_start(sb_scal[:], scal.ap()), 16)
        d_aux.inc(s.dma_start(r2l_sb[:], aux_r2l.ap()), 16)
        d_aux.inc(s.dma_start(r2r_sb[:], aux_r2r.ap()), 16)
    blk.sync(em)

    def em(g):
        d_wkv.inc(g.dma_start(
            wkv_sb[:].rearrange("p (t c) -> p t c", t=DT),
            w_qkv.ap()[:, H * Dh:H * Dh + 256].rearrange("(t p) c -> p t c", p=128)),
            16)
        for h in range(2):
            d_wq[h].inc(g.dma_start(
                wq_sl[h][:].rearrange("p (t c) -> p t c", t=DT),
                w_qkv.ap()[:, h * 128:(h + 1) * 128]
                .rearrange("(t p) c -> p t c", p=128)), 16)
            d_wq[h].inc(g.dma_start(
                bias_sb[:, h * 80:(h + 1) * 80],
                aux_bias.ap()[:, h * 80:(h + 1) * 80]), 16)
    blk.gpsimd(em)

    def em(s):
        d_aux.inc(s.dma_start(vones_sb[:], aux_vones.ap()), 16)
    blk.sync(em)

    def em(v):
        d_aux.wait(v)
        c_misc.inc(v.tensor_tensor(sb_usds[:], sb_scal[:, 1:2], sb_scal[:, 2:3],
                                   ALU.mult))
        c_misc.inc(v.tensor_copy(
            V_sb[:].rearrange("p (t c) -> p t c", t=CTXT)[:, :, 128:129],
            vones_sb[:].rearrange("p (t c) -> p t c", c=1)))
    blk.vector(em)

    ln_done = {}
    kv_t_done = {}
    kv_e_done = {}
    k_ready = {}
    v_ready = {}
    ln_seq = []
    kv_seq = []

    KP = P67[:, 512:1024]
    VP = P67[:, 512:640]

    def emit_ln(t):
        sl = len(ln_seq) % 2
        prev2 = ln_seq[-2] if len(ln_seq) >= 2 else None
        prev1 = ln_seq[-1] if len(ln_seq) >= 1 else None
        ln_seq.append(t)

        def em(g, t=t, sl=sl, prev2=prev2):
            if prev2 is not None:
                c_ln_v.wait(g, ln_done[prev2])
            d_x[sl].inc(g.dma_start(xbuf[sl][:],
                                    x_ctx.ap()[t * 128:(t + 1) * 128, :]), 16)
        blk.gpsimd(em)

        def em(v, t=t, sl=sl):
            d_x[sl].wait(v)
            for k in range(8):
                c_ln_v.inc(v.bn_stats(stt[:, 6 * k:6 * k + 6],
                                      xbuf[sl][:, 512 * k:512 * (k + 1)]))
            c_ln_v.wait(v)
            c_ln_v.inc(v.bn_aggr(mv[:], stt[:]))
            c_ln_v.wait(v)
            c_ln_v.inc(v.tensor_scalar_add(veps[:], mv[:, 1:2], EPS))
        blk.vector(em)

        def em(sc):
            c_ln_v.wait(sc)
            c_ln_s.inc(sc.activation(lnl[:], veps[:], AF.Ln))
            c_ln_s.wait(sc)
            c_ln_s.inc(sc.activation(rstd[:], lnl[:], AF.Exp, scale=-0.5))
        blk.scalar(em)

        def em(v):
            c_ln_s.wait(v)
            c_ln_v.inc(v.tensor_scalar(nbias[:], mv[:, 0:1], rstd[:], -1.0,
                                       ALU.mult, ALU.mult))
        blk.vector(em)

        def em(v, t=t, sl=sl, prev1=prev1):
            c_ln_v.wait(v)
            if prev1 is not None:
                d_xnw[prev1 // 4].wait(v, 16 * ((prev1 % 4) + 1))
            c_ln_v.inc(v.tensor_scalar(xnbuf[0][:], xbuf[sl][:], rstd[:],
                                       nbias[:], ALU.mult, ALU.add))
            ln_done[t] = c_ln_v.n
        blk.vector(em)

        def em(sc, t=t, sl=sl):
            c_ln_v.wait(sc)
            d_xnw[t // 4].inc(
                sc.dma_start(xn_sp.ap()[t * 128:(t + 1) * 128, :], xnbuf[0][:]), 16)
        blk.scalar(em)

        def em(g):
            emit_casts(g, 2)
        blk.gpsimd(em)

    def emit_kv(tc):
        xt = xt_sl[0]
        prev = kv_seq[-1] if kv_seq else None
        kv_seq.append(tc)

        def em(s, tc=tc, xt=xt, prev=prev):
            d_xnw[tc].wait(s, 16 * 4)
            if prev is not None:
                c_kv_t.wait(s, kv_t_done[prev])
            for dt in range(DT):
                d_xt[0].inc(s.dma_start(
                    xt[:, dt * 512:dt * 512 + 512],
                    xn_sp.ap()[tc * 512:(tc + 1) * 512, dt * 128:(dt + 1) * 128],
                    transpose=True), 16)
        blk.sync(em)

        def em(t, tc=tc, xt=xt, prev=prev):
            d_xt[0].wait(t)
            d_wkv.wait(t)
            if tc == kv_seq[0]:
                c_misc.wait(t)
            if prev is not None:
                c_kv_e.wait(t, k_ready[prev])
            for dt in range(DT):
                c_kv_t.inc(t.matmul(
                    KP, wkv_sb[:, dt * 256:dt * 256 + 128],
                    xt[:, dt * 512:dt * 512 + 512],
                    start=(dt == 0), stop=(dt == DT - 1)))
        blk.tensor(em)

        def em(v, tc=tc):
            c_kv_t.wait(v)
            c_kv_e.inc(v.tensor_copy(K_sb[:, tc * 512:(tc + 1) * 512], KP))
            k_ready[tc] = c_kv_e.n
        blk.vector(em)

        for j in range(4):
            def emt(t, tc=tc, j=j):
                c_kv_e.wait(t)
                for dt in range(DT):
                    c_kv_t.inc(t.matmul(
                        VP, xt[:, dt * 512 + j * 128:dt * 512 + (j + 1) * 128],
                        wkv_sb[:, dt * 256 + 128:dt * 256 + 256],
                        start=(dt == 0), stop=(dt == DT - 1)))
            blk.tensor(emt)

            def emv(v, tc=tc, j=j):
                c_kv_t.wait(v)
                c_misc.wait(v, 2)
                c_kv_e.inc(v.tensor_copy(
                    V_sb[:, (4 * tc + j) * 129:(4 * tc + j) * 129 + 128], VP))
            blk.vector(emv)
        kv_t_done[tc] = c_kv_t.n
        kv_e_done[tc] = c_kv_e.n
        v_ready[tc] = c_kv_e.n

        def em(g):
            emit_casts(g, 4)
        blk.gpsimd(em)

    HEADS_AT = {tc: [] for tc in range(10)}
    for h in range(H):
        HEADS_AT[max(0, LO[h][0] // 4)].append(h)

    PHASE_A_PLAN = [True]

    def emit_phase_a():
        for tc in range(9, -1, -1):
            for t in range(4 * tc, 4 * tc + 4):
                emit_ln(t)
            emit_kv(tc)
            if tc == 8:
                def em(s):
                    for q in (8, 9):
                        d_xnw[q].wait(s, 16 * 4)
                    for dt in range(DT):
                        for hf in range(2):
                            d_xn2.inc(s.dma_start(
                                xn_own[:, dt * 1024 + hf * 512:
                                       dt * 1024 + hf * 512 + 512],
                                xn_sp.ap()[4096 + hf * 512:
                                           4096 + hf * 512 + 512,
                                           dt * 128:(dt + 1) * 128],
                                transpose=True), 16)
                blk.sync(em)
            for h in HEADS_AT.get(tc, []):
                emit_head(h)


    sc_free = {}
    wq_ex_used = {}
    pv_free = {}
    probs_free = {}
    as_used = {}
    wq_used = {}
    qe_done = {}
    qsl_free = {}

    SCORE_PS = [P01[:, 0:512], P01[:, 512:1024], P67[:, 0:512]]
    NSC = 3
    PV_PS = [P23[:, 0:129], P23[:, 512:641], P45[:, 0:129], P45[:, 512:641]]
    Q_PS = [P67[:, 0:512], P67[:, 0:512]]
    as_ctr = [0]

    def emit_head(h):
        wsl = h % 2

        def em(g, h=h, wsl=wsl):
            if h >= 2:
                c_q.wait(g, wq_used[wsl])
                c_ex.wait(g, wq_ex_used.get(wsl, 0))
                d_wq[wsl].inc(g.dma_start(
                    wq_sl[wsl][:].rearrange("p (t c) -> p t c", t=DT),
                    w_qkv.ap()[:, h * 128:(h + 1) * 128]
                    .rearrange("(t p) c -> p t c", p=128)), 16)
                d_wq[wsl].inc(g.dma_start(
                    bias_sb[:, wsl * 80:(wsl + 1) * 80],
                    aux_bias.ap()[:, h * 80:(h + 1) * 80]), 16)
            emit_casts(g, 3)
        blk.gpsimd(em)

        for half in range(2):
            def em(t, h=h, wsl=wsl, half=half):
                d_wq[wsl].wait(t, 32 * (h // 2 + 1))
                if h == 0 and half == 0:
                    c_misc.wait(t)
                    d_aux.wait(t)
                    d_xn2.wait(t)
                if half == 1:
                    c_qe.wait(t, qe_done[(h, 0)])
                elif h >= 1:
                    c_qe.wait(t, qe_done[(h - 1, 1)])
                    c_ex.wait(t)
                for dt in range(DT):
                    c_q.inc(t.matmul(
                        Q_PS[half],
                        wq_sl[wsl][:, dt * 128:(dt + 1) * 128],
                        xn_own[:, dt * 1024 + half * 512:dt * 1024 + half * 512 + 512],
                        start=(dt == 0), stop=(dt == DT - 1)))
                if half == 1:
                    wq_used[wsl] = c_q.n
            blk.tensor(em)

            def em(v, h=h, half=half):
                c_q.wait(v)
                if h >= 2 and half == 0:
                    c_sc.wait(v, qsl_free[h % 2])
                c_qe.inc(v.tensor_copy(q_sl[h % 2][:, half * 512:half * 512 + 512],
                                       Q_PS[half]))
                qe_done[(h, half)] = c_qe.n
            blk.vector(em)

        ex_at = {}
        nx = c_ex.n
        for w in range(2):
            for kt in range(36 + 4 * w - 1, LO[h][w] - 1, -1):
                nx += 1
                ex_at[(w, kt)] = nx

        for w in range(2):
            lo, hi = LO[h][w], 36 + 4 * w
            ndiag = 32 + 4 * w
            kts = list(range(hi - 1, lo - 1, -1))
            sc_at = {}
            mk_at = {}
            pv_at = {}

            def emt(t, h=h, w=w, kts=kts, lo=lo, ndiag=ndiag):
                def scores(idx):
                    kt = kts[idx]
                    ssl = idx % NSC
                    js = kt - ndiag
                    coloff = 128 * js if js > 0 else 0
                    sps = SCORE_PS[ssl]
                    if ("sc", ssl) in sc_free:
                        c_ex.wait(t, sc_free[("sc", ssl)])
                    c_kv_e.wait(t, k_ready[kt // 4])
                    if w == 0 and idx == 0:
                        c_qe.wait(t, qe_done[(h, 1)])
                    t.matmul(sps[:, coloff:512],
                             K_sb[:, kt * 128:(kt + 1) * 128],
                             q_sl[h % 2][:, 512 * w + coloff:512 * (w + 1)],
                             start=True, stop=False)
                    c_sc.inc(t.matmul(
                        sps[:, coloff:512],
                        r2l_sb[:, h * 128:(h + 1) * 128],
                        r2r_sb[:, coloff:512],
                        start=False, stop=True))
                    sc_at[kt] = c_sc.n
                    sc_free[("sc", ssl)] = ex_at[(w, kt)]

                def pv(idx):
                    kt = kts[idx]
                    psl = idx % 4
                    js = kt - ndiag
                    c_ex.wait(t, ex_at[(w, kt)])
                    c_kv_e.wait(t, v_ready[kt // 4])
                    for ts in range(max(0, js), 4):
                        first = (kt == min(ndiag + ts, hi - 1))
                        if first and ("pv", ts) in pv_free:
                            c_pve.wait(t, pv_free[("pv", ts)])
                        c_pv.inc(t.matmul(
                            PV_PS[ts],
                            probs[psl][:, ts * 128:(ts + 1) * 128],
                            V_sb[:, kt * 129:(kt + 1) * 129],
                            start=first, stop=(kt == lo)))
                    pv_at[kt] = c_pv.n

                for idx in range(len(kts)):
                    scores(idx)
                    if idx >= 2:
                        pv(idx - 2)
                for idx in (len(kts) - 2, len(kts) - 1):
                    if idx >= 0:
                        pv(idx)
            blk.tensor(emt)

            def emv(v, w=w, kts=kts, ndiag=ndiag):
                for idx, kt in enumerate(kts):
                    js = kt - ndiag
                    if js < 0:
                        continue
                    coloff = 128 * js
                    sps = SCORE_PS[idx % NSC]
                    c_sc.wait(v, sc_at[kt])
                    c_mk.inc(v.tensor_tensor(
                        sps[:, coloff:coloff + 128],
                        sps[:, coloff:coloff + 128], sb_tri[:], ALU.add))
                    mk_at[kt] = c_mk.n
            blk.vector(emv)

            def ems(sc, h=h, w=w, kts=kts, ndiag=ndiag):
                for idx, kt in enumerate(kts):
                    psl = idx % 4
                    js = kt - ndiag
                    coloff = 128 * js if js > 0 else 0
                    sps = SCORE_PS[idx % NSC]
                    if ("pb", psl) in probs_free:
                        c_pv.wait(sc, probs_free[("pb", psl)])
                    if idx >= 4:
                        c_pv.wait(sc, pv_at.get(kts[idx - 4], 0))
                    c_sc.wait(sc, sc_at[kt])
                    if js >= 0:
                        c_mk.wait(sc, mk_at[kt])
                    d_wq[h % 2].wait(sc, 32 * (h // 2 + 1))
                    c_ex.inc(sc.activation(
                        probs[psl][:, coloff:512], sps[:, coloff:512], AF.Exp,
                        bias=bias_sb[:, (h % 2) * 80 + w * 40 + kt:
                                     (h % 2) * 80 + w * 40 + kt + 1],
                        scale=SCALE))
                    assert c_ex.n == ex_at[(w, kt)]
            blk.scalar(ems)
            for idx, kt in enumerate(kts):
                probs_free[("pb", idx % 4)] = pv_at[kt]
            if w == 1:
                qsl_free[h % 2] = c_sc.n
                wq_ex_used[h % 2] = c_ex.n
            pv_after = c_pv.n

            for ts in range(4):
                asl = as_ctr[0] % 2
                as_ctr[0] += 1

                def em(v, ts=ts, asl=asl, pa=pv_after):
                    c_pv.wait(v, pa)
                    if ("as", asl) in as_used:
                        d_as[asl].wait(v, as_used[("as", asl)])
                    c_pve.inc(v.reciprocal(rec_sb[:, ts:ts + 1],
                                           PV_PS[ts][:, 128:129]))
                    c_pve.wait(v)
                    c_pve.inc(v.tensor_scalar(
                        astage[asl][:], PV_PS[ts][:, 0:128],
                        rec_sb[:, ts:ts + 1], None, ALU.mult))
                blk.vector(em)
                pv_free[("pv", ts)] = c_pve.n
                pve_after = c_pve.n

                def em(g, ts=ts, asl=asl, h=h, w=w, pa=pve_after):
                    c_pve.wait(g, pa)
                    d_as[asl].inc(g.dma_start(
                        attn_sp.ap()[w * 512 + ts * 128:w * 512 + (ts + 1) * 128,
                                     h * 128:(h + 1) * 128],
                        astage[asl][:]), 16)
                blk.gpsimd(em)
                as_used[("as", asl)] = d_as[asl].n

    emit_phase_a()

    att_scope.close()
    xt_scope.close()
    wkv_scope.close()
    ln_scope.close()
    kv_scope.close()

    wc_scope = ExitStack()
    we = wc_scope.enter_context
    attnT = we(nc.sbuf_tensor("attnT", [128, DT * 1024], BF16, side="right"))
    wcq_sl = [we(nc.sbuf_tensor(f"wcq_sl{i}", [128, DT * 256], BF16, side="right"))
              for i in range(2)]
    xo_sl = [we(nc.sbuf_tensor(f"xo_sl{i}", [128, 256], F32, side="right")) for i in range(2)]
    wstage = [we(nc.sbuf_tensor(f"wstage{i}", [128, 256], F32, side="right")) for i in range(2)]

    def em(s):
        for i in range(2):
            d_as[i].wait(s, d_as[i].n)
        c_sc.wait(s)
        c_ex.wait(s)
        c_pv.wait(s)
        c_pve.wait(s)
        c_q.wait(s)
        c_qe.wait(s)
        c_ln_v.wait(s)
        c_kv_t.wait(s)
        c_kv_e.wait(s)
        c_ln_s.wait(s)
        for ein in range(DT):
            d_at.inc(s.dma_start(
                attnT[:, ein * 1024:(ein + 1) * 1024],
                attn_sp.ap()[:, ein * 128:(ein + 1) * 128],
                transpose=True), 16)
    blk.sync(em)

    wcq_used = {}
    xo_used = {}
    wst_used = {}
    WC_PS = [P01[:, 0:256], P01[:, 512:768], P23[:, 0:256], P23[:, 512:768],
             P45[:, 0:256], P45[:, 512:768], P67[:, 0:256], P67[:, 512:768]]
    wc_bank_used = {}

    for eg in range(WCG):
        qsl = eg % 2

        def em(g, eg=eg, qsl=qsl):
            if eg >= 2:
                c_wc.wait(g, wcq_used[qsl])
            else:
                c_sc.wait(g)
                c_pv.wait(g)
                c_pve.wait(g)
                c_ex.wait(g)
            d_wcq[qsl].inc(g.dma_start(
                wcq_sl[qsl][:].rearrange("p (t c) -> p t c", t=DT),
                w_c.ap()[:, eg * 256:(eg + 1) * 256]
                .rearrange("(t p) c -> p t c", p=128)), 16)
        blk.gpsimd(em)

        for tt in range(8):
            bank = (eg * 8 + tt) % 8
            xsl = (eg * 8 + tt) % 2
            wsl2 = (eg * 8 + tt) % 2

            def em(s, eg=eg, tt=tt, xsl=xsl):
                if ("xo", xsl) in xo_used:
                    c_wce.wait(s, xo_used[("xo", xsl)])
                d_xo[xsl].inc(s.dma_start(
                    xo_sl[xsl][:],
                    x_ctx.ap()[4096 + tt * 128:4096 + (tt + 1) * 128,
                               eg * 256:(eg + 1) * 256]), 16)
            blk.sync(em)

            def em(t, eg=eg, tt=tt, qsl=qsl, bank=bank):
                d_at.wait(t)
                d_wcq[qsl].wait(t, 16 * (eg // 2 + 1))
                if ("wb", bank) in wc_bank_used:
                    c_wce.wait(t, wc_bank_used[("wb", bank)])
                for ein in range(DT):
                    c_wc.inc(t.matmul(
                        WC_PS[bank],
                        attnT[:, ein * 1024 + tt * 128:ein * 1024 + (tt + 1) * 128],
                        wcq_sl[qsl][:, ein * 256:(ein + 1) * 256],
                        start=(ein == 0), stop=(ein == DT - 1)))
            blk.tensor(em)
            wcq_used[qsl] = c_wc.n
            wc_after = c_wc.n

            def em(v, bank=bank, xsl=xsl, wsl2=wsl2, wa=wc_after):
                c_wc.wait(v, wa)
                d_xo[xsl].wait(v)
                if ("ws", wsl2) in wst_used:
                    tt2, n2 = wst_used[("ws", wsl2)]
                    d_out[tt2].wait(v, n2)
                c_wce.inc(v.scalar_tensor_tensor(
                    wstage[wsl2][:], WC_PS[bank], 1.0, xo_sl[xsl][:],
                    ALU.mult, ALU.add))
            blk.vector(em)
            wc_bank_used[("wb", bank)] = c_wce.n
            xo_used[("xo", xsl)] = c_wce.n
            wce_after = c_wce.n

            def em(s, eg=eg, tt=tt, wsl2=wsl2, wa=wce_after):
                c_wce.wait(s, wa)
                d_out[tt].inc(s.dma_start(
                    out.ap()[tt * 128:(tt + 1) * 128, eg * 256:(eg + 1) * 256],
                    wstage[wsl2][:]), 16)
            blk.sync(em)
            wst_used[("ws", wsl2)] = (tt, d_out[tt].n)

    wc_scope.close()
    wc_out_base = [d_out[tt].n for tt in range(8)]

    ffn_scope = ExitStack()
    fe = ffn_scope.enter_context
    m_sb = fe(nc.sbuf_tensor("m_sb", [128, 15 * 1024], BF16, side="right"))
    wg_sl = [fe(nc.sbuf_tensor(f"wg_sl{i}", [128, DT * 512], BF16, side="right"))
             for i in range(2)]
    dwt_sl = [fe(nc.sbuf_tensor(f"dwt_sl{i}", [128, 15 * 512], BF16, side="right"))
              for i in range(2)]
    silu_st = [fe(nc.sbuf_tensor(f"silu_st{i}", [128, 1024], F32, side="right"))
               for i in range(2)]
    dstage = [fe(nc.sbuf_tensor(f"dstage{i}", [128, 512], F32, side="right")) for i in range(2)]

    def em(s):
        for tt in range(8):
            d_out[tt].wait(s, d_out[tt].n)
        c_wc.wait(s)
        c_wce.wait(s)
        d_gsc.wait(s)
        for dt in range(DT):
            for hf in range(2):
                d_xn2.inc(s.dma_start(
                    xn_own[:, dt * 1024 + hf * 512:dt * 1024 + hf * 512 + 512],
                    xn_sp.ap()[4096 + hf * 512:4096 + hf * 512 + 512,
                               dt * 128:(dt + 1) * 128],
                    transpose=True), 16)
    blk.sync(em)

    def em(g):
        d_gsc.wait(g)
    blk.gpsimd(em)

    GU_PS = [(P01, P23), (P45, P67)]
    wg_used = {}
    dwt_used = {}
    sl_used = {}
    dst_used = {}
    silu_done = {}
    m_done = {}
    first_ffn = [True]

    for ic in range(6):
        nit = IC_SIZES[ic]
        it0 = IC_BASE[ic]
        dn_base = c_dn.n
        dne_base = c_dne.n

        groups = []
        q0 = 0
        while q0 < nit:
            n = min(4, nit - q0)
            groups.append((q0, n))
            q0 += n

        for gi, (q0, qn) in enumerate(groups):
            for mat in (0, 1):
                wsl = mat
                src = gsc if mat == 0 else usc

                def em(s, wsl=wsl, src=src, it0=it0, q0=q0, qn=qn):
                    if ("wg", wsl) in wg_used:
                        c_gu.wait(s, wg_used[("wg", wsl)])
                    d_xn2.wait(s)
                    if qn == 4:
                        d_wg[wsl].inc(s.dma_start(
                            wg_sl[wsl][:].rearrange("p (t c) -> p t c", c=512),
                            src.ap()[:, (it0 + q0) * 128:(it0 + q0 + qn) * 128]
                            .rearrange("(t p) c -> p t c", p=128)), 16)
                    else:
                        for dt in range(DT):
                            d_wg[wsl].inc(s.dma_start(
                                wg_sl[wsl][:].rearrange("p (t c) -> p t c", t=DT)
                                [:, dt, 0:qn * 128],
                                src.ap()[dt * 128:(dt + 1) * 128,
                                         (it0 + q0) * 128:
                                         (it0 + q0 + qn) * 128]), 16)
                blk.sync(em)
                if mat == 0:
                    wg_loads_g = d_wg[0].n
                else:
                    wg_loads_u = d_wg[1].n

            for itl in range(qn):
                it = q0 + itl
                par = it % 2
                PG, PU = GU_PS[par]

                def em(t, itl=itl, par=par, PG=PG, wl=wg_loads_g, dneb=dne_base):
                    d_wg[0].wait(t, wl)
                    if first_ffn[0]:
                        c_wce.wait(t)
                        first_ffn[0] = False
                    c_dne.wait(t, dneb)
                    if ("si", par) in silu_done:
                        c_sl.wait(t, silu_done[("si", par)])
                    for hf in range(2):
                        for dt in range(DT):
                            c_gu.inc(t.matmul(
                                PG[:, hf * 512:hf * 512 + 512],
                                wg_sl[0][:, dt * 512 + itl * 128:
                                          dt * 512 + (itl + 1) * 128],
                                xn_own[:, dt * 1024 + hf * 512:
                                       dt * 1024 + hf * 512 + 512],
                                start=(dt == 0), stop=(dt == DT - 1)))
                blk.tensor(em)
                ga = c_gu.n

                def em(sc, par=par, PG=PG, ga=ga, it=it):
                    c_gu.wait(sc, ga)
                    d_aux.wait(sc, 96)
                    ssl = it % 2
                    if ("sl", ssl) in sl_used:
                        c_m.wait(sc, sl_used[("sl", ssl)])
                    c_sl.inc(sc.activation(silu_st[ssl][:], PG[:],
                                           AF.Silu, scale=sb_scal[:, 0:1]))
                blk.scalar(em)
                silu_done[("si", par)] = c_sl.n

                def em(t, itl=itl, par=par, PU=PU, wl=wg_loads_u, dneb=dne_base):
                    d_wg[1].wait(t, wl)
                    c_dne.wait(t, dneb)
                    if ("mu", par) in m_done:
                        c_m.wait(t, m_done[("mu", par)])
                    for hf in range(2):
                        for dt in range(DT):
                            c_gu.inc(t.matmul(
                                PU[:, hf * 512:hf * 512 + 512],
                                wg_sl[1][:, dt * 512 + itl * 128:
                                          dt * 512 + (itl + 1) * 128],
                                xn_own[:, dt * 1024 + hf * 512:
                                       dt * 1024 + hf * 512 + 512],
                                start=(dt == 0), stop=(dt == DT - 1)))
                blk.tensor(em)
                ua = c_gu.n

                def em(v, par=par, PU=PU, ua=ua, it=it, q0=q0, itl=itl,
                       db=dn_base):
                    c_gu.wait(v, ua)
                    c_dn.wait(v, db)
                    ssl = it % 2
                    c_sl.wait(v, silu_done[("si", par)])
                    c_m.inc(v.tensor_tensor(
                        m_sb[:, (q0 + itl) * 1024:(q0 + itl + 1) * 1024],
                        PU[:], silu_st[ssl][:], ALU.mult))
                blk.vector(em)
                sl_used[("sl", it % 2)] = c_m.n
                m_done[("mu", par)] = c_m.n
            wg_used[("wg", 0)] = c_gu.n
            wg_used[("wg", 1)] = c_gu.n
        m_all = c_m.n

        for e8 in range(DEG):
            dsl = e8 % 2

            def em(s, e8=e8, dsl=dsl, nit=nit, it0=it0):
                if ("dw", dsl) in dwt_used:
                    c_dn.wait(s, dwt_used[("dw", dsl)])
                for itl in range(0, nit, 2):
                    d_dw[dsl].inc(s.dma_start(
                        dwt_sl[dsl][:, itl * 512:(itl + 1) * 512],
                        dsc.ap()[(it0 + itl) * 128:(it0 + itl + 1) * 128,
                                 e8 * 512:(e8 + 1) * 512]), 16)
            blk.sync(em)

            def em(sc, e8=e8, dsl=dsl, nit=nit, it0=it0):
                if ("dw", dsl) in dwt_used:
                    c_dn.wait(sc, dwt_used[("dw", dsl)])
                for itl in range(1, nit, 2):
                    d_dw[dsl].inc(sc.dma_start(
                        dwt_sl[dsl][:, itl * 512:(itl + 1) * 512],
                        dsc.ap()[(it0 + itl) * 128:(it0 + itl + 1) * 128,
                                 e8 * 512:(e8 + 1) * 512]), 16)
            blk.scalar(em)
            dw_loads = d_dw[dsl].n

            for tt in range(8):
                bank = tt % 2
                bslice = [P01[:, 0:512], P23[:, 0:512]][bank]

                def em(t, tt=tt, dsl=dsl, nit=nit, bank=bank, bslice=bslice,
                       ma=m_all, dl=dw_loads):
                    d_dw[dsl].wait(t, dl)
                    c_m.wait(t, ma)
                    if ("db", bank) in dst_used:
                        c_dne.wait(t, dst_used[("db", bank)])
                    for itl in range(nit):
                        c_dn.inc(t.matmul(
                            bslice,
                            m_sb[:, itl * 1024 + tt * 128:itl * 1024 + (tt + 1) * 128],
                            dwt_sl[dsl][:, itl * 512:(itl + 1) * 512],
                            start=(itl == 0), stop=(itl == nit - 1)))
                blk.tensor(em)
                da = c_dn.n

                def em(v, bank=bank, bslice=bslice, da=da):
                    c_dn.wait(v, da)
                    c_misc.wait(v, 2)
                    if ("ds", bank) in dst_used:
                        tt2, n2 = dst_used[("ds", bank)]
                        d_outa[tt2].wait(v, n2)
                    c_dne.inc(v.tensor_scalar(
                        dstage[bank][:], bslice, sb_usds[:], None, ALU.mult))
                blk.vector(em)
                dst_used[("db", bank)] = c_dne.n
                dnea = c_dne.n

                def em(g, ic=ic, e8=e8, tt=tt, bank=bank, da=dnea):
                    c_dne.wait(g, da)
                    d_out[tt].wait(g, wc_out_base[tt])
                    d_outa[tt].wait(g, 16 * DEG * ic)
                    d_outa[tt].inc(g.dma_start(
                        out.ap()[tt * 128:(tt + 1) * 128, e8 * 512:(e8 + 1) * 512],
                        dstage[bank][:], accum_op=ALU.add), 16)
                blk.gpsimd(em)
                dst_used[("ds", bank)] = (tt, d_outa[tt].n)
            dwt_used[("dw", dsl)] = c_dn.n

    def em(g):
        if DEBUG_DUMP:
            c_m.wait(g)
            g.dma_start(dbg_m.ap(), m_sb[:]).then_inc(d_wkv.h, 16)
            d_wkv.wait(g, d_wkv.n + 16)
    blk.gpsimd(em)

    ffn_scope.close()

    def em(g):
        for tt in range(8):
            d_outa[tt].wait(g, d_outa[tt].n)
        for sm in [d_gsc, d_wkv] + d_wq + d_as + d_wcq:
            sm.wait(g, sm.n)
        if DEBUG_DUMP:
            g.dma_start(dbg_xn.ap(), xn_sp.ap()).then_inc(d_gsc.h, 16)
            g.dma_start(dbg_at.ap(), attn_sp.ap()).then_inc(d_gsc.h, 16)
            d_gsc.wait(g, d_gsc.n + 32)
    blk.gpsimd(em)

    def em(s):
        for tt in range(8):
            d_out[tt].wait(s, d_out[tt].n)
        for sm in [d_aux, d_at, d_xn2] + d_x + d_xt + d_xo + d_wg + d_dw:
            sm.wait(s, sm.n)
    blk.sync(em)

    def em(sc):
        for sm in d_xnw:
            sm.wait(sc, sm.n)
    blk.scalar(em)

    st.close()
    return nc


_NC_CACHE = []


def kernel(L0_attn_hidden_states, L0_attn_w_qkv, L0_attn_w_c,
           L0_ffn_gate_w, L0_ffn_gate_s, L0_ffn_up_w, L0_ffn_up_s,
           L0_ffn_down_w, L0_ffn_down_s):
    x = np.ascontiguousarray(np.asarray(L0_attn_hidden_states, dtype=np.float32))
    w_qkv = np.ascontiguousarray(np.asarray(L0_attn_w_qkv, dtype=np.float32))
    w_c = np.ascontiguousarray(np.asarray(L0_attn_w_c, dtype=np.float32))

    def i8view(w):
        w = np.ascontiguousarray(np.asarray(w, dtype=np.int32))
        return np.ascontiguousarray(w.view(np.int8)[:, 0::4])

    gw8 = np.ascontiguousarray(i8view(L0_ffn_gate_w).T)
    uw8 = np.ascontiguousarray(i8view(L0_ffn_up_w).T)
    dw8 = np.ascontiguousarray(i8view(L0_ffn_down_w).T)

    scal = np.empty((128, 3), np.float32)
    scal[:, 0] = np.asarray(L0_ffn_gate_s, np.float32).reshape(())
    scal[:, 1] = np.asarray(L0_ffn_up_s, np.float32).reshape(())
    scal[:, 2] = np.asarray(L0_ffn_down_s, np.float32).reshape(())

    ii = np.arange(128, dtype=np.float64)
    r2l = np.empty((5, H * 128), np.float64)
    for h in range(H):
        A = float(SLOPES[h]) / SCALE
        u = A * ii
        u_hi = u.astype(ml_dtypes.bfloat16).astype(np.float64)
        u_lo = u - u_hi
        A_hi = np.float64(np.asarray(A, dtype=ml_dtypes.bfloat16).astype(np.float64))
        A_lo = A - A_hi
        r2l[0, h * 128:(h + 1) * 128] = u_hi
        r2l[1, h * 128:(h + 1) * 128] = u_lo
        r2l[2, h * 128:(h + 1) * 128] = A_hi
        r2l[3, h * 128:(h + 1) * 128] = A_hi
        r2l[4, h * 128:(h + 1) * 128] = A_lo
    r2l = np.concatenate([r2l, np.zeros((123, H * 128))]).astype(ml_dtypes.bfloat16)
    jj = np.arange(512, dtype=np.float64)
    jh = 4.0 * np.floor(jj / 4.0)
    jl = jj - jh
    r2r = np.concatenate([np.stack([np.ones(512), np.ones(512), -jh, -jl, -jh]),
                          np.zeros((123, 512))]).astype(ml_dtypes.bfloat16)
    tri = np.where(np.arange(128)[None, :] >= np.arange(128)[:, None],
                   0.0, NEG).astype(np.float32)
    bias_tab = np.zeros((128, H * 80), np.float32)
    for h in range(H):
        for w in range(2):
            for kt in range(40):
                bias_tab[:, h * 80 + w * 40 + kt] = (
                    float(SLOPES[h]) * (128.0 * kt - 4096.0 - 512.0 * w))
    ident = np.eye(128, dtype=ml_dtypes.bfloat16)

    in_maps = []
    for c in range(8):
        b, p = c // 4, c % 4
        b0 = 1024 * p
        xctx = np.zeros((CTX, E), np.float32)
        lo_abs = b0 - 4096
        nreal = b0 + 1024 - max(0, lo_abs)
        xctx[CTX - nreal:] = x[b, max(0, lo_abs):b0 + 1024]
        npad = 32 - 8 * p
        vones = np.zeros((128, CTXT), np.float32)
        vones[:, npad:] = 1.0
        in_maps.append(dict(
            x_ctx=xctx, w_qkv=w_qkv, w_c=w_c, gw8=gw8, uw8=uw8, dw8=dw8,
            scal=scal, aux_r2l=r2l, aux_r2r=r2r, aux_vones=vones,
            aux_bias=bias_tab, aux_tri=tri, aux_id=ident))

    if not _NC_CACHE:
        _NC_CACHE.append(build_nc())
    nc = _NC_CACHE[0]
    res = run_bass_kernel_spmd(nc, in_maps, core_ids=list(range(8)))

    outp = np.empty((B, S, E), np.float32)
    for c in range(8):
        b, p = c // 4, c % 4
        outp[b, 1024 * p:1024 * (p + 1)] = res.results[c]["out"]
    return outp


# version 13
# speedup vs baseline: 1.0342x; 1.0342x over previous
import math
from contextlib import ExitStack

import numpy as np
import ml_dtypes

import concourse.bass as bass
import concourse.mybir as mybir
from concourse.bass_utils import run_bass_kernel_spmd

F32 = mybir.dt.float32
BF16 = mybir.dt.bfloat16
I8 = mybir.dt.int8
AF = mybir.ActivationFunctionType
ALU = mybir.AluOpType

B, S, E, H, Dh = 2, 4096, 4096, 32, 128
FFI = 11008
SCALE = 1.0 / math.sqrt(128.0)
TOK = 1024
CTX = 5120
CTXT = 40
DT = 32
ITT = 86
EPS = 1e-5
NEG = -1.0e30

IC_SIZES = [15, 15, 14, 14, 14, 14]
IC_BASE = [0, 15, 30, 44, 58, 72]
DEBUG_DUMP = False
WCG = 16
DEG = 8


def _alibi_slopes(n: int) -> np.ndarray:
    cp2 = 2 ** int(math.floor(math.log2(n)))
    base = 2.0 ** (-(2.0 ** (-(math.log2(cp2) - 3))))
    slopes = base ** np.arange(1, cp2 + 1, dtype=np.float64)
    if cp2 != n:
        extra_base = 2.0 ** (-(2.0 ** (-(math.log2(2 * cp2) - 3))))
        rem = min(cp2, n - cp2)
        slopes = np.concatenate(
            [slopes, extra_base ** np.arange(1, 1 + 2 * rem, 2, dtype=np.float64)])
    return slopes.astype(np.float32)


SLOPES = _alibi_slopes(H)


def _lo(h, w):
    lo = 0
    for kt in range(36 + 4 * w):
        if SLOPES[h] * (128 * (kt + 1) - 1 - 4096 - 512 * w) < -150.0:
            lo = kt + 1
    return lo


LO = [[_lo(h, w) for w in range(2)] for h in range(H)]
for h in range(H):
    for w in range(2):
        assert LO[h][w] <= 32 + 4 * w


class Sem:
    def __init__(self, nc, st, name):
        self.h = st.enter_context(nc.semaphore(name))
        self.n = 0
        self.last_wait = {}

    def inc(self, instr, k=1):
        instr.then_inc(self.h, k)
        self.n += k
        return instr

    def wait(self, eng, val=None):
        v = self.n if val is None else val
        q = type(eng).__name__
        if self.last_wait.get(q, -1) >= v:
            return
        self.last_wait[q] = v
        eng.wait_ge(self.h, v)


def build_nc():
    nc = bass.Bass("TRN2", target_bir_lowering=False)

    x_ctx = nc.dram_tensor("x_ctx", [CTX, E], F32, kind="ExternalInput")
    w_qkv = nc.dram_tensor("w_qkv", [E, H * Dh + 2 * Dh], F32, kind="ExternalInput")
    w_c = nc.dram_tensor("w_c", [E, E], F32, kind="ExternalInput")
    gw8 = nc.dram_tensor("gw8", [E, FFI], I8, kind="ExternalInput")
    uw8 = nc.dram_tensor("uw8", [E, FFI], I8, kind="ExternalInput")
    dw8 = nc.dram_tensor("dw8", [FFI, E], I8, kind="ExternalInput")
    scal = nc.dram_tensor("scal", [128, 3], F32, kind="ExternalInput")
    aux_r2l = nc.dram_tensor("aux_r2l", [128, H * 128], BF16, kind="ExternalInput")
    aux_r2r = nc.dram_tensor("aux_r2r", [128, 512], BF16, kind="ExternalInput")
    aux_vones = nc.dram_tensor("aux_vones", [128, CTXT], F32, kind="ExternalInput")
    aux_bias = nc.dram_tensor("aux_bias", [128, H * 80], F32, kind="ExternalInput")
    aux_tri = nc.dram_tensor("aux_tri", [128, 128], F32, kind="ExternalInput")
    aux_id = nc.dram_tensor("aux_id", [128, 128], BF16, kind="ExternalInput")

    out = nc.dram_tensor("out", [TOK, E], F32, kind="ExternalOutput")
    if DEBUG_DUMP:
        dbg_xn = nc.dram_tensor("dbg_xn", [CTX, E], BF16, kind="ExternalOutput")
        dbg_at = nc.dram_tensor("dbg_at", [TOK, E], BF16, kind="ExternalOutput")
        dbg_k = nc.dram_tensor("dbg_k", [128, CTX], BF16, kind="ExternalOutput")
        dbg_m = nc.dram_tensor("dbg_m", [128, 15 * 1024], BF16, kind="ExternalOutput")

    xn_sp = nc.dram_tensor("xn_sp", [CTX, E], BF16, kind="Internal")
    attn_sp = nc.dram_tensor("attn_sp", [TOK, E], BF16, kind="Internal")
    gsc = nc.dram_tensor("gsc", [E, FFI], BF16, kind="Internal")
    usc = nc.dram_tensor("usc", [E, FFI], BF16, kind="Internal")
    dsc = nc.dram_tensor("dsc", [FFI, E], BF16, kind="Internal")

    st = ExitStack()
    e = st.enter_context

    sb_scal = e(nc.sbuf_tensor("sb_scal", [128, 3], F32))
    sb_usds = e(nc.sbuf_tensor("sb_usds", [128, 1], F32))
    sb_tri = e(nc.sbuf_tensor("sb_tri", [128, 128], F32))
    sb_id = e(nc.sbuf_tensor("sb_id", [128, 128], BF16))
    xn_own = e(nc.sbuf_tensor("xn_own", [128, DT * 1024], BF16))

    d_aux = Sem(nc, st, "d_aux")
    d_x = [Sem(nc, st, f"d_x{i}") for i in range(2)]
    d_xnw = [Sem(nc, st, f"d_xnw{i}") for i in range(10)]
    d_xt = [Sem(nc, st, f"d_xt{i}") for i in range(2)]
    d_wkv = Sem(nc, st, "d_wkv")
    d_gsc = Sem(nc, st, "d_gsc")
    d_wq = [Sem(nc, st, f"d_wq{i}") for i in range(2)]
    d_as = [Sem(nc, st, f"d_as{i}") for i in range(2)]
    d_at = Sem(nc, st, "d_at")
    d_wcq = [Sem(nc, st, f"d_wcq{i}") for i in range(2)]
    d_xo = [Sem(nc, st, f"d_xo{i}") for i in range(2)]
    d_out = [Sem(nc, st, f"d_out{i}") for i in range(8)]
    d_outa = [Sem(nc, st, f"d_outa{i}") for i in range(8)]
    d_xn2 = Sem(nc, st, "d_xn2")
    d_wg = [Sem(nc, st, f"d_wg{i}") for i in range(2)]
    d_dw = [Sem(nc, st, f"d_dw{i}") for i in range(2)]

    c_ln_v = Sem(nc, st, "c_ln_v")
    c_ln_s = Sem(nc, st, "c_ln_s")
    c_kv_t = Sem(nc, st, "c_kv_t")
    c_kv_e = Sem(nc, st, "c_kv_e")
    c_q = Sem(nc, st, "c_q")
    c_qe = Sem(nc, st, "c_qe")
    c_sc = Sem(nc, st, "c_sc")
    c_mk = Sem(nc, st, "c_mk")
    c_ex = Sem(nc, st, "c_ex")
    c_pv = Sem(nc, st, "c_pv")
    c_pve = Sem(nc, st, "c_pve")
    c_wc = Sem(nc, st, "c_wc")
    c_wce = Sem(nc, st, "c_wce")
    c_gu = Sem(nc, st, "c_gu")
    c_sl = Sem(nc, st, "c_sl")
    c_m = Sem(nc, st, "c_m")
    c_dn = Sem(nc, st, "c_dn")
    c_dne = Sem(nc, st, "c_dne")
    c_misc = Sem(nc, st, "c_misc")

    P01 = e(nc.psum_tensor("P01", [128, 1024], F32))
    P23 = e(nc.psum_tensor("P23", [128, 1024], F32))
    P45 = e(nc.psum_tensor("P45", [128, 1024], F32))
    P67 = e(nc.psum_tensor("P67", [128, 1024], F32))

    blk = st.enter_context(nc.Block())

    kv_scope = ExitStack()
    ke = kv_scope.enter_context
    K_sb = ke(nc.sbuf_tensor("K_sb", [128, CTX], BF16, side="right"))
    V_sb = ke(nc.sbuf_tensor("V_sb", [128, CTXT * 129], BF16, side="right"))
    vones_sb = ke(nc.sbuf_tensor("vones_sb", [128, CTXT], F32, side="right"))

    ln_scope = ExitStack()
    le = ln_scope.enter_context
    xbuf = [le(nc.sbuf_tensor(f"xbuf{i}", [128, E], F32, side="right")) for i in range(2)]
    xnbuf = [le(nc.sbuf_tensor("xnbuf0", [128, E], BF16, side="right"))]
    stt = le(nc.sbuf_tensor("stt", [128, 48], F32, side="right"))
    mv = le(nc.sbuf_tensor("mv", [128, 2], F32, side="right"))
    veps = le(nc.sbuf_tensor("veps", [128, 1], F32, side="right"))
    lnl = le(nc.sbuf_tensor("lnl", [128, 1], F32, side="right"))
    rstd = le(nc.sbuf_tensor("rstd", [128, 1], F32, side="right"))
    nbias = le(nc.sbuf_tensor("nbias", [128, 1], F32, side="right"))

    wkv_scope = ExitStack()
    wkv_sb = wkv_scope.enter_context(nc.sbuf_tensor("wkv_sb", [128, DT * 256], BF16, side="right"))
    xt_scope = ExitStack()
    xt_sl = [xt_scope.enter_context(
        nc.sbuf_tensor("xt_sl0", [128, DT * 512], BF16, side="right"))]

    cast_jobs = []
    for r in range(DT):
        cast_jobs.append((gsc.ap()[r * 128:(r + 1) * 128, :],
                          gw8.ap()[r * 128:(r + 1) * 128, :]))
    for r in range(DT):
        cast_jobs.append((usc.ap()[r * 128:(r + 1) * 128, :],
                          uw8.ap()[r * 128:(r + 1) * 128, :]))
    for r in range(ITT):
        cast_jobs.append((dsc.ap()[r * 128:(r + 1) * 128, :],
                          dw8.ap()[r * 128:(r + 1) * 128, :]))
    cast_pos = [0]

    def emit_casts(g, k):
        for _ in range(k):
            if cast_pos[0] < len(cast_jobs):
                dst, src = cast_jobs[cast_pos[0]]
                d_gsc.inc(g.dma_start(dst, src), 16)
                cast_pos[0] += 1

    att_scope = ExitStack()
    ae = att_scope.enter_context
    wq_sl = [ae(nc.sbuf_tensor(f"wq_sl{i}", [128, DT * 128], BF16, side="right")) for i in range(2)]
    q_sl = [ae(nc.sbuf_tensor(f"q_sl{i}", [128, 1024], BF16, side="right")) for i in range(2)]
    probs = [ae(nc.sbuf_tensor(f"probs{i}", [128, 512], BF16, side="right")) for i in range(4)]
    rec_sb = ae(nc.sbuf_tensor("rec_sb", [128, 4], F32, side="right"))
    astage = [ae(nc.sbuf_tensor(f"astage{i}", [128, 128], BF16, side="right")) for i in range(2)]
    r2l_sb = ae(nc.sbuf_tensor("r2l_sb", [128, H * 128], BF16, side="right"))
    r2r_sb = ae(nc.sbuf_tensor("r2r_sb", [128, 512], BF16, side="right"))
    bias_sb = ae(nc.sbuf_tensor("bias_sb", [128, 2 * 80], F32, side="right"))

    def em(s):
        d_aux.inc(s.dma_start(sb_tri[:], aux_tri.ap()), 16)
        d_aux.inc(s.dma_start(sb_id[:], aux_id.ap()), 16)
        d_aux.inc(s.dma_start(sb_scal[:], scal.ap()), 16)
        d_aux.inc(s.dma_start(r2l_sb[:], aux_r2l.ap()), 16)
        d_aux.inc(s.dma_start(r2r_sb[:], aux_r2r.ap()), 16)
    blk.sync(em)

    def em(g):
        d_wkv.inc(g.dma_start(
            wkv_sb[:].rearrange("p (t c) -> p t c", t=DT),
            w_qkv.ap()[:, H * Dh:H * Dh + 256].rearrange("(t p) c -> p t c", p=128)),
            16)
        for h in range(2):
            d_wq[h].inc(g.dma_start(
                wq_sl[h][:].rearrange("p (t c) -> p t c", t=DT),
                w_qkv.ap()[:, h * 128:(h + 1) * 128]
                .rearrange("(t p) c -> p t c", p=128)), 16)
            d_wq[h].inc(g.dma_start(
                bias_sb[:, h * 80:(h + 1) * 80],
                aux_bias.ap()[:, h * 80:(h + 1) * 80]), 16)
    blk.gpsimd(em)

    def em(s):
        d_aux.inc(s.dma_start(vones_sb[:], aux_vones.ap()), 16)
    blk.sync(em)

    def em(v):
        d_aux.wait(v)
        c_misc.inc(v.tensor_tensor(sb_usds[:], sb_scal[:, 1:2], sb_scal[:, 2:3],
                                   ALU.mult))
        c_misc.inc(v.tensor_copy(
            V_sb[:].rearrange("p (t c) -> p t c", t=CTXT)[:, :, 128:129],
            vones_sb[:].rearrange("p (t c) -> p t c", c=1)))
    blk.vector(em)

    ln_done = {}
    kv_t_done = {}
    kv_e_done = {}
    k_ready = {}
    v_ready = {}
    ln_seq = []
    kv_seq = []

    KP = P67[:, 512:1024]
    VP = P67[:, 512:640]

    def emit_ln(t):
        sl = len(ln_seq) % 2
        prev2 = ln_seq[-2] if len(ln_seq) >= 2 else None
        prev1 = ln_seq[-1] if len(ln_seq) >= 1 else None
        ln_seq.append(t)

        def em(g, t=t, sl=sl, prev2=prev2):
            if prev2 is not None:
                c_ln_v.wait(g, ln_done[prev2])
            d_x[sl].inc(g.dma_start(xbuf[sl][:],
                                    x_ctx.ap()[t * 128:(t + 1) * 128, :]), 16)
        blk.gpsimd(em)

        def em(v, t=t, sl=sl):
            d_x[sl].wait(v)
            for k in range(8):
                c_ln_v.inc(v.bn_stats(stt[:, 6 * k:6 * k + 6],
                                      xbuf[sl][:, 512 * k:512 * (k + 1)]))
            c_ln_v.wait(v)
            c_ln_v.inc(v.bn_aggr(mv[:], stt[:]))
            c_ln_v.wait(v)
            c_ln_v.inc(v.tensor_scalar_add(veps[:], mv[:, 1:2], EPS))
        blk.vector(em)

        def em(sc):
            c_ln_v.wait(sc)
            c_ln_s.inc(sc.activation(lnl[:], veps[:], AF.Ln))
            c_ln_s.wait(sc)
            c_ln_s.inc(sc.activation(rstd[:], lnl[:], AF.Exp, scale=-0.5))
        blk.scalar(em)

        def em(v):
            c_ln_s.wait(v)
            c_ln_v.inc(v.tensor_scalar(nbias[:], mv[:, 0:1], rstd[:], -1.0,
                                       ALU.mult, ALU.mult))
        blk.vector(em)

        def em(g, t=t, sl=sl, prev1=prev1):
            c_ln_v.wait(g)
            if prev1 is not None:
                d_xnw[prev1 // 4].wait(g, 16 * ((prev1 % 4) + 1))
            c_ln_v.inc(g.tensor_scalar(xnbuf[0][:], xbuf[sl][:], rstd[:],
                                       nbias[:], ALU.mult, ALU.add))
            ln_done[t] = c_ln_v.n
        blk.gpsimd(em)

        def em(sc, t=t, sl=sl):
            c_ln_v.wait(sc)
            d_xnw[t // 4].inc(
                sc.dma_start(xn_sp.ap()[t * 128:(t + 1) * 128, :], xnbuf[0][:]), 16)
        blk.scalar(em)

        def em(g):
            emit_casts(g, 2)
        blk.gpsimd(em)

    def emit_kv(tc):
        xt = xt_sl[0]
        prev = kv_seq[-1] if kv_seq else None
        kv_seq.append(tc)

        def em(s, tc=tc, xt=xt, prev=prev):
            d_xnw[tc].wait(s, 16 * 4)
            if prev is not None:
                c_kv_t.wait(s, kv_t_done[prev])
            for dt in range(DT):
                d_xt[0].inc(s.dma_start(
                    xt[:, dt * 512:dt * 512 + 512],
                    xn_sp.ap()[tc * 512:(tc + 1) * 512, dt * 128:(dt + 1) * 128],
                    transpose=True), 16)
        blk.sync(em)

        def em(t, tc=tc, xt=xt, prev=prev):
            d_xt[0].wait(t)
            d_wkv.wait(t)
            if tc == kv_seq[0]:
                c_misc.wait(t)
            if prev is not None:
                c_kv_e.wait(t, k_ready[prev])
            for dt in range(DT):
                c_kv_t.inc(t.matmul(
                    KP, wkv_sb[:, dt * 256:dt * 256 + 128],
                    xt[:, dt * 512:dt * 512 + 512],
                    start=(dt == 0), stop=(dt == DT - 1)))
        blk.tensor(em)

        def em(v, tc=tc):
            c_kv_t.wait(v)
            c_kv_e.inc(v.tensor_copy(K_sb[:, tc * 512:(tc + 1) * 512], KP))
            k_ready[tc] = c_kv_e.n
        blk.vector(em)

        for j in range(4):
            def emt(t, tc=tc, j=j):
                c_kv_e.wait(t)
                for dt in range(DT):
                    c_kv_t.inc(t.matmul(
                        VP, xt[:, dt * 512 + j * 128:dt * 512 + (j + 1) * 128],
                        wkv_sb[:, dt * 256 + 128:dt * 256 + 256],
                        start=(dt == 0), stop=(dt == DT - 1)))
            blk.tensor(emt)

            def emv(v, tc=tc, j=j):
                c_kv_t.wait(v)
                c_misc.wait(v, 2)
                c_kv_e.inc(v.tensor_copy(
                    V_sb[:, (4 * tc + j) * 129:(4 * tc + j) * 129 + 128], VP))
            blk.vector(emv)
        kv_t_done[tc] = c_kv_t.n
        kv_e_done[tc] = c_kv_e.n
        v_ready[tc] = c_kv_e.n

        def em(g):
            emit_casts(g, 4)
        blk.gpsimd(em)

    HEADS_AT = {tc: [] for tc in range(10)}
    for h in range(H):
        HEADS_AT[max(0, LO[h][0] // 4)].append(h)

    PHASE_A_PLAN = [True]

    def emit_phase_a():
        for tc in range(9, -1, -1):
            for t in range(4 * tc, 4 * tc + 4):
                emit_ln(t)
            emit_kv(tc)
            if tc == 8:
                def em(s):
                    for q in (8, 9):
                        d_xnw[q].wait(s, 16 * 4)
                    for dt in range(DT):
                        for hf in range(2):
                            d_xn2.inc(s.dma_start(
                                xn_own[:, dt * 1024 + hf * 512:
                                       dt * 1024 + hf * 512 + 512],
                                xn_sp.ap()[4096 + hf * 512:
                                           4096 + hf * 512 + 512,
                                           dt * 128:(dt + 1) * 128],
                                transpose=True), 16)
                blk.sync(em)
            for h in HEADS_AT.get(tc, []):
                emit_head(h)


    sc_free = {}
    wq_ex_used = {}
    pv_free = {}
    probs_free = {}
    as_used = {}
    wq_used = {}
    qe_done = {}
    qsl_free = {}

    SCORE_PS = [P01[:, 0:512], P01[:, 512:1024], P67[:, 0:512]]
    NSC = 3
    PV_PS = [P23[:, 0:129], P23[:, 512:641], P45[:, 0:129], P45[:, 512:641]]
    Q_PS = [P67[:, 0:512], P67[:, 0:512]]
    as_ctr = [0]

    def emit_head(h):
        wsl = h % 2

        def em(g, h=h, wsl=wsl):
            if h >= 2:
                c_q.wait(g, wq_used[wsl])
                c_ex.wait(g, wq_ex_used.get(wsl, 0))
                d_wq[wsl].inc(g.dma_start(
                    wq_sl[wsl][:].rearrange("p (t c) -> p t c", t=DT),
                    w_qkv.ap()[:, h * 128:(h + 1) * 128]
                    .rearrange("(t p) c -> p t c", p=128)), 16)
                d_wq[wsl].inc(g.dma_start(
                    bias_sb[:, wsl * 80:(wsl + 1) * 80],
                    aux_bias.ap()[:, h * 80:(h + 1) * 80]), 16)
            emit_casts(g, 3)
        blk.gpsimd(em)

        for half in range(2):
            def em(t, h=h, wsl=wsl, half=half):
                d_wq[wsl].wait(t, 32 * (h // 2 + 1))
                if h == 0 and half == 0:
                    c_misc.wait(t)
                    d_aux.wait(t)
                    d_xn2.wait(t)
                if half == 1:
                    c_qe.wait(t, qe_done[(h, 0)])
                elif h >= 1:
                    c_qe.wait(t, qe_done[(h - 1, 1)])
                    c_ex.wait(t)
                for dt in range(DT):
                    c_q.inc(t.matmul(
                        Q_PS[half],
                        wq_sl[wsl][:, dt * 128:(dt + 1) * 128],
                        xn_own[:, dt * 1024 + half * 512:dt * 1024 + half * 512 + 512],
                        start=(dt == 0), stop=(dt == DT - 1)))
                if half == 1:
                    wq_used[wsl] = c_q.n
            blk.tensor(em)

            def em(v, h=h, half=half):
                c_q.wait(v)
                if h >= 2 and half == 0:
                    c_sc.wait(v, qsl_free[h % 2])
                c_qe.inc(v.tensor_copy(q_sl[h % 2][:, half * 512:half * 512 + 512],
                                       Q_PS[half]))
                qe_done[(h, half)] = c_qe.n
            blk.vector(em)

        ex_at = {}
        nx = c_ex.n
        for w in range(2):
            for kt in range(36 + 4 * w - 1, LO[h][w] - 1, -1):
                nx += 1
                ex_at[(w, kt)] = nx

        for w in range(2):
            lo, hi = LO[h][w], 36 + 4 * w
            ndiag = 32 + 4 * w
            kts = list(range(hi - 1, lo - 1, -1))
            sc_at = {}
            mk_at = {}
            pv_at = {}

            def emt(t, h=h, w=w, kts=kts, lo=lo, ndiag=ndiag):
                def scores(idx):
                    kt = kts[idx]
                    ssl = idx % NSC
                    js = kt - ndiag
                    coloff = 128 * js if js > 0 else 0
                    sps = SCORE_PS[ssl]
                    if ("sc", ssl) in sc_free:
                        c_ex.wait(t, sc_free[("sc", ssl)])
                    c_kv_e.wait(t, k_ready[kt // 4])
                    if w == 0 and idx == 0:
                        c_qe.wait(t, qe_done[(h, 1)])
                    t.matmul(sps[:, coloff:512],
                             K_sb[:, kt * 128:(kt + 1) * 128],
                             q_sl[h % 2][:, 512 * w + coloff:512 * (w + 1)],
                             start=True, stop=False)
                    c_sc.inc(t.matmul(
                        sps[:, coloff:512],
                        r2l_sb[:, h * 128:(h + 1) * 128],
                        r2r_sb[:, coloff:512],
                        start=False, stop=True))
                    sc_at[kt] = c_sc.n
                    sc_free[("sc", ssl)] = ex_at[(w, kt)]

                def pv(idx):
                    kt = kts[idx]
                    psl = idx % 4
                    js = kt - ndiag
                    c_ex.wait(t, ex_at[(w, kt)])
                    c_kv_e.wait(t, v_ready[kt // 4])
                    for ts in range(max(0, js), 4):
                        first = (kt == min(ndiag + ts, hi - 1))
                        if first and ("pv", ts) in pv_free:
                            c_pve.wait(t, pv_free[("pv", ts)])
                        c_pv.inc(t.matmul(
                            PV_PS[ts],
                            probs[psl][:, ts * 128:(ts + 1) * 128],
                            V_sb[:, kt * 129:(kt + 1) * 129],
                            start=first, stop=(kt == lo)))
                    pv_at[kt] = c_pv.n

                for idx in range(len(kts)):
                    scores(idx)
                    if idx >= 2:
                        pv(idx - 2)
                for idx in (len(kts) - 2, len(kts) - 1):
                    if idx >= 0:
                        pv(idx)
            blk.tensor(emt)

            def emv(v, w=w, kts=kts, ndiag=ndiag):
                for idx, kt in enumerate(kts):
                    js = kt - ndiag
                    if js < 0:
                        continue
                    coloff = 128 * js
                    sps = SCORE_PS[idx % NSC]
                    c_sc.wait(v, sc_at[kt])
                    c_mk.inc(v.tensor_tensor(
                        sps[:, coloff:coloff + 128],
                        sps[:, coloff:coloff + 128], sb_tri[:], ALU.add))
                    mk_at[kt] = c_mk.n
            blk.vector(emv)

            def ems(sc, h=h, w=w, kts=kts, ndiag=ndiag):
                for idx, kt in enumerate(kts):
                    psl = idx % 4
                    js = kt - ndiag
                    coloff = 128 * js if js > 0 else 0
                    sps = SCORE_PS[idx % NSC]
                    if ("pb", psl) in probs_free:
                        c_pv.wait(sc, probs_free[("pb", psl)])
                    if idx >= 4:
                        c_pv.wait(sc, pv_at.get(kts[idx - 4], 0))
                    c_sc.wait(sc, sc_at[kt])
                    if js >= 0:
                        c_mk.wait(sc, mk_at[kt])
                    d_wq[h % 2].wait(sc, 32 * (h // 2 + 1))
                    c_ex.inc(sc.activation(
                        probs[psl][:, coloff:512], sps[:, coloff:512], AF.Exp,
                        bias=bias_sb[:, (h % 2) * 80 + w * 40 + kt:
                                     (h % 2) * 80 + w * 40 + kt + 1],
                        scale=SCALE))
                    assert c_ex.n == ex_at[(w, kt)]
            blk.scalar(ems)
            for idx, kt in enumerate(kts):
                probs_free[("pb", idx % 4)] = pv_at[kt]
            if w == 1:
                qsl_free[h % 2] = c_sc.n
                wq_ex_used[h % 2] = c_ex.n
            pv_after = c_pv.n

            for ts in range(4):
                asl = as_ctr[0] % 2
                as_ctr[0] += 1

                def em(v, ts=ts, asl=asl, pa=pv_after):
                    c_pv.wait(v, pa)
                    if ("as", asl) in as_used:
                        d_as[asl].wait(v, as_used[("as", asl)])
                    c_pve.inc(v.reciprocal(rec_sb[:, ts:ts + 1],
                                           PV_PS[ts][:, 128:129]))
                    c_pve.wait(v)
                    c_pve.inc(v.tensor_scalar(
                        astage[asl][:], PV_PS[ts][:, 0:128],
                        rec_sb[:, ts:ts + 1], None, ALU.mult))
                blk.vector(em)
                pv_free[("pv", ts)] = c_pve.n
                pve_after = c_pve.n

                def em(g, ts=ts, asl=asl, h=h, w=w, pa=pve_after):
                    c_pve.wait(g, pa)
                    d_as[asl].inc(g.dma_start(
                        attn_sp.ap()[w * 512 + ts * 128:w * 512 + (ts + 1) * 128,
                                     h * 128:(h + 1) * 128],
                        astage[asl][:]), 16)
                blk.gpsimd(em)
                as_used[("as", asl)] = d_as[asl].n

    emit_phase_a()

    att_scope.close()
    xt_scope.close()
    wkv_scope.close()
    ln_scope.close()
    kv_scope.close()

    wc_scope = ExitStack()
    we = wc_scope.enter_context
    attnT = we(nc.sbuf_tensor("attnT", [128, DT * 1024], BF16, side="right"))
    wcq_sl = [we(nc.sbuf_tensor(f"wcq_sl{i}", [128, DT * 256], BF16, side="right"))
              for i in range(2)]
    xo_sl = [we(nc.sbuf_tensor(f"xo_sl{i}", [128, 256], F32, side="right")) for i in range(2)]
    wstage = [we(nc.sbuf_tensor(f"wstage{i}", [128, 256], F32, side="right")) for i in range(2)]

    def em(s):
        for i in range(2):
            d_as[i].wait(s, d_as[i].n)
        c_sc.wait(s)
        c_ex.wait(s)
        c_pv.wait(s)
        c_pve.wait(s)
        c_q.wait(s)
        c_qe.wait(s)
        c_ln_v.wait(s)
        c_kv_t.wait(s)
        c_kv_e.wait(s)
        c_ln_s.wait(s)
        for ein in range(DT):
            d_at.inc(s.dma_start(
                attnT[:, ein * 1024:(ein + 1) * 1024],
                attn_sp.ap()[:, ein * 128:(ein + 1) * 128],
                transpose=True), 16)
    blk.sync(em)

    wcq_used = {}
    xo_used = {}
    wst_used = {}
    WC_PS = [P01[:, 0:256], P01[:, 512:768], P23[:, 0:256], P23[:, 512:768],
             P45[:, 0:256], P45[:, 512:768], P67[:, 0:256], P67[:, 512:768]]
    wc_bank_used = {}

    for eg in range(WCG):
        qsl = eg % 2

        def em(g, eg=eg, qsl=qsl):
            if eg >= 2:
                c_wc.wait(g, wcq_used[qsl])
            else:
                c_sc.wait(g)
                c_pv.wait(g)
                c_pve.wait(g)
                c_ex.wait(g)
            d_wcq[qsl].inc(g.dma_start(
                wcq_sl[qsl][:].rearrange("p (t c) -> p t c", t=DT),
                w_c.ap()[:, eg * 256:(eg + 1) * 256]
                .rearrange("(t p) c -> p t c", p=128)), 16)
        blk.gpsimd(em)

        for tt in range(8):
            bank = (eg * 8 + tt) % 8
            xsl = (eg * 8 + tt) % 2
            wsl2 = (eg * 8 + tt) % 2

            def em(s, eg=eg, tt=tt, xsl=xsl):
                if ("xo", xsl) in xo_used:
                    c_wce.wait(s, xo_used[("xo", xsl)])
                d_xo[xsl].inc(s.dma_start(
                    xo_sl[xsl][:],
                    x_ctx.ap()[4096 + tt * 128:4096 + (tt + 1) * 128,
                               eg * 256:(eg + 1) * 256]), 16)
            blk.sync(em)

            def em(t, eg=eg, tt=tt, qsl=qsl, bank=bank):
                d_at.wait(t)
                d_wcq[qsl].wait(t, 16 * (eg // 2 + 1))
                if ("wb", bank) in wc_bank_used:
                    c_wce.wait(t, wc_bank_used[("wb", bank)])
                for ein in range(DT):
                    c_wc.inc(t.matmul(
                        WC_PS[bank],
                        attnT[:, ein * 1024 + tt * 128:ein * 1024 + (tt + 1) * 128],
                        wcq_sl[qsl][:, ein * 256:(ein + 1) * 256],
                        start=(ein == 0), stop=(ein == DT - 1)))
            blk.tensor(em)
            wcq_used[qsl] = c_wc.n
            wc_after = c_wc.n

            def em(v, bank=bank, xsl=xsl, wsl2=wsl2, wa=wc_after):
                c_wc.wait(v, wa)
                d_xo[xsl].wait(v)
                if ("ws", wsl2) in wst_used:
                    tt2, n2 = wst_used[("ws", wsl2)]
                    d_out[tt2].wait(v, n2)
                c_wce.inc(v.scalar_tensor_tensor(
                    wstage[wsl2][:], WC_PS[bank], 1.0, xo_sl[xsl][:],
                    ALU.mult, ALU.add))
            blk.vector(em)
            wc_bank_used[("wb", bank)] = c_wce.n
            xo_used[("xo", xsl)] = c_wce.n
            wce_after = c_wce.n

            def em(s, eg=eg, tt=tt, wsl2=wsl2, wa=wce_after):
                c_wce.wait(s, wa)
                d_out[tt].inc(s.dma_start(
                    out.ap()[tt * 128:(tt + 1) * 128, eg * 256:(eg + 1) * 256],
                    wstage[wsl2][:]), 16)
            blk.sync(em)
            wst_used[("ws", wsl2)] = (tt, d_out[tt].n)

    wc_scope.close()
    wc_out_base = [d_out[tt].n for tt in range(8)]

    ffn_scope = ExitStack()
    fe = ffn_scope.enter_context
    m_sb = fe(nc.sbuf_tensor("m_sb", [128, 15 * 1024], BF16, side="right"))
    wg_sl = [fe(nc.sbuf_tensor(f"wg_sl{i}", [128, DT * 512], BF16, side="right"))
             for i in range(2)]
    dwt_sl = [fe(nc.sbuf_tensor(f"dwt_sl{i}", [128, 15 * 512], BF16, side="right"))
              for i in range(2)]
    silu_st = [fe(nc.sbuf_tensor(f"silu_st{i}", [128, 1024], F32, side="right"))
               for i in range(2)]
    dstage = [fe(nc.sbuf_tensor(f"dstage{i}", [128, 512], F32, side="right")) for i in range(2)]

    def em(s):
        for tt in range(8):
            d_out[tt].wait(s, d_out[tt].n)
        c_wc.wait(s)
        c_wce.wait(s)
        d_gsc.wait(s)
        for dt in range(DT):
            for hf in range(2):
                d_xn2.inc(s.dma_start(
                    xn_own[:, dt * 1024 + hf * 512:dt * 1024 + hf * 512 + 512],
                    xn_sp.ap()[4096 + hf * 512:4096 + hf * 512 + 512,
                               dt * 128:(dt + 1) * 128],
                    transpose=True), 16)
    blk.sync(em)

    def em(g):
        d_gsc.wait(g)
    blk.gpsimd(em)

    GU_PS = [(P01, P23), (P45, P67)]
    wg_used = {}
    dwt_used = {}
    sl_used = {}
    dst_used = {}
    silu_done = {}
    m_done = {}
    first_ffn = [True]

    for ic in range(6):
        nit = IC_SIZES[ic]
        it0 = IC_BASE[ic]
        dn_base = c_dn.n
        dne_base = c_dne.n

        groups = []
        q0 = 0
        while q0 < nit:
            n = min(4, nit - q0)
            groups.append((q0, n))
            q0 += n

        for gi, (q0, qn) in enumerate(groups):
            for mat in (0, 1):
                wsl = mat
                src = gsc if mat == 0 else usc

                def em(s, wsl=wsl, src=src, it0=it0, q0=q0, qn=qn):
                    if ("wg", wsl) in wg_used:
                        c_gu.wait(s, wg_used[("wg", wsl)])
                    d_xn2.wait(s)
                    if qn == 4:
                        d_wg[wsl].inc(s.dma_start(
                            wg_sl[wsl][:].rearrange("p (t c) -> p t c", c=512),
                            src.ap()[:, (it0 + q0) * 128:(it0 + q0 + qn) * 128]
                            .rearrange("(t p) c -> p t c", p=128)), 16)
                    else:
                        for dt in range(DT):
                            d_wg[wsl].inc(s.dma_start(
                                wg_sl[wsl][:].rearrange("p (t c) -> p t c", t=DT)
                                [:, dt, 0:qn * 128],
                                src.ap()[dt * 128:(dt + 1) * 128,
                                         (it0 + q0) * 128:
                                         (it0 + q0 + qn) * 128]), 16)
                blk.sync(em)
                if mat == 0:
                    wg_loads_g = d_wg[0].n
                else:
                    wg_loads_u = d_wg[1].n

            for itl in range(qn):
                it = q0 + itl
                par = it % 2
                PG, PU = GU_PS[par]

                def em(t, itl=itl, par=par, PG=PG, wl=wg_loads_g, dneb=dne_base):
                    d_wg[0].wait(t, wl)
                    if first_ffn[0]:
                        c_wce.wait(t)
                        first_ffn[0] = False
                    c_dne.wait(t, dneb)
                    if ("si", par) in silu_done:
                        c_sl.wait(t, silu_done[("si", par)])
                    for hf in range(2):
                        for dt in range(DT):
                            c_gu.inc(t.matmul(
                                PG[:, hf * 512:hf * 512 + 512],
                                wg_sl[0][:, dt * 512 + itl * 128:
                                          dt * 512 + (itl + 1) * 128],
                                xn_own[:, dt * 1024 + hf * 512:
                                       dt * 1024 + hf * 512 + 512],
                                start=(dt == 0), stop=(dt == DT - 1)))
                blk.tensor(em)
                ga = c_gu.n

                def em(sc, par=par, PG=PG, ga=ga, it=it):
                    c_gu.wait(sc, ga)
                    d_aux.wait(sc, 96)
                    ssl = it % 2
                    if ("sl", ssl) in sl_used:
                        c_m.wait(sc, sl_used[("sl", ssl)])
                    c_sl.inc(sc.activation(silu_st[ssl][:], PG[:],
                                           AF.Silu, scale=sb_scal[:, 0:1]))
                blk.scalar(em)
                silu_done[("si", par)] = c_sl.n

                def em(t, itl=itl, par=par, PU=PU, wl=wg_loads_u, dneb=dne_base):
                    d_wg[1].wait(t, wl)
                    c_dne.wait(t, dneb)
                    if ("mu", par) in m_done:
                        c_m.wait(t, m_done[("mu", par)])
                    for hf in range(2):
                        for dt in range(DT):
                            c_gu.inc(t.matmul(
                                PU[:, hf * 512:hf * 512 + 512],
                                wg_sl[1][:, dt * 512 + itl * 128:
                                          dt * 512 + (itl + 1) * 128],
                                xn_own[:, dt * 1024 + hf * 512:
                                       dt * 1024 + hf * 512 + 512],
                                start=(dt == 0), stop=(dt == DT - 1)))
                blk.tensor(em)
                ua = c_gu.n

                def em(v, par=par, PU=PU, ua=ua, it=it, q0=q0, itl=itl,
                       db=dn_base):
                    c_gu.wait(v, ua)
                    c_dn.wait(v, db)
                    ssl = it % 2
                    c_sl.wait(v, silu_done[("si", par)])
                    c_m.inc(v.tensor_tensor(
                        m_sb[:, (q0 + itl) * 1024:(q0 + itl + 1) * 1024],
                        PU[:], silu_st[ssl][:], ALU.mult))
                blk.vector(em)
                sl_used[("sl", it % 2)] = c_m.n
                m_done[("mu", par)] = c_m.n
            wg_used[("wg", 0)] = c_gu.n
            wg_used[("wg", 1)] = c_gu.n
        m_all = c_m.n

        for e8 in range(DEG):
            dsl = e8 % 2

            def em(s, e8=e8, dsl=dsl, nit=nit, it0=it0):
                if ("dw", dsl) in dwt_used:
                    c_dn.wait(s, dwt_used[("dw", dsl)])
                for itl in range(0, nit, 2):
                    d_dw[dsl].inc(s.dma_start(
                        dwt_sl[dsl][:, itl * 512:(itl + 1) * 512],
                        dsc.ap()[(it0 + itl) * 128:(it0 + itl + 1) * 128,
                                 e8 * 512:(e8 + 1) * 512]), 16)
            blk.sync(em)

            def em(sc, e8=e8, dsl=dsl, nit=nit, it0=it0):
                if ("dw", dsl) in dwt_used:
                    c_dn.wait(sc, dwt_used[("dw", dsl)])
                for itl in range(1, nit, 2):
                    d_dw[dsl].inc(sc.dma_start(
                        dwt_sl[dsl][:, itl * 512:(itl + 1) * 512],
                        dsc.ap()[(it0 + itl) * 128:(it0 + itl + 1) * 128,
                                 e8 * 512:(e8 + 1) * 512]), 16)
            blk.scalar(em)
            dw_loads = d_dw[dsl].n

            for tt in range(8):
                bank = tt % 2
                bslice = [P01[:, 0:512], P23[:, 0:512]][bank]

                def em(t, tt=tt, dsl=dsl, nit=nit, bank=bank, bslice=bslice,
                       ma=m_all, dl=dw_loads):
                    d_dw[dsl].wait(t, dl)
                    c_m.wait(t, ma)
                    if ("db", bank) in dst_used:
                        c_dne.wait(t, dst_used[("db", bank)])
                    for itl in range(nit):
                        c_dn.inc(t.matmul(
                            bslice,
                            m_sb[:, itl * 1024 + tt * 128:itl * 1024 + (tt + 1) * 128],
                            dwt_sl[dsl][:, itl * 512:(itl + 1) * 512],
                            start=(itl == 0), stop=(itl == nit - 1)))
                blk.tensor(em)
                da = c_dn.n

                def em(v, bank=bank, bslice=bslice, da=da):
                    c_dn.wait(v, da)
                    c_misc.wait(v, 2)
                    if ("ds", bank) in dst_used:
                        tt2, n2 = dst_used[("ds", bank)]
                        d_outa[tt2].wait(v, n2)
                    c_dne.inc(v.tensor_scalar(
                        dstage[bank][:], bslice, sb_usds[:], None, ALU.mult))
                blk.vector(em)
                dst_used[("db", bank)] = c_dne.n
                dnea = c_dne.n

                def em(g, ic=ic, e8=e8, tt=tt, bank=bank, da=dnea):
                    c_dne.wait(g, da)
                    d_out[tt].wait(g, wc_out_base[tt])
                    d_outa[tt].wait(g, 16 * DEG * ic)
                    d_outa[tt].inc(g.dma_start(
                        out.ap()[tt * 128:(tt + 1) * 128, e8 * 512:(e8 + 1) * 512],
                        dstage[bank][:], accum_op=ALU.add), 16)
                blk.gpsimd(em)
                dst_used[("ds", bank)] = (tt, d_outa[tt].n)
            dwt_used[("dw", dsl)] = c_dn.n

    def em(g):
        if DEBUG_DUMP:
            c_m.wait(g)
            g.dma_start(dbg_m.ap(), m_sb[:]).then_inc(d_wkv.h, 16)
            d_wkv.wait(g, d_wkv.n + 16)
    blk.gpsimd(em)

    ffn_scope.close()

    def em(g):
        for tt in range(8):
            d_outa[tt].wait(g, d_outa[tt].n)
        for sm in [d_gsc, d_wkv] + d_wq + d_as + d_wcq:
            sm.wait(g, sm.n)
        if DEBUG_DUMP:
            g.dma_start(dbg_xn.ap(), xn_sp.ap()).then_inc(d_gsc.h, 16)
            g.dma_start(dbg_at.ap(), attn_sp.ap()).then_inc(d_gsc.h, 16)
            d_gsc.wait(g, d_gsc.n + 32)
    blk.gpsimd(em)

    def em(s):
        for tt in range(8):
            d_out[tt].wait(s, d_out[tt].n)
        for sm in [d_aux, d_at, d_xn2] + d_x + d_xt + d_xo + d_wg + d_dw:
            sm.wait(s, sm.n)
    blk.sync(em)

    def em(sc):
        for sm in d_xnw:
            sm.wait(sc, sm.n)
    blk.scalar(em)

    st.close()
    return nc


_NC_CACHE = []


def kernel(L0_attn_hidden_states, L0_attn_w_qkv, L0_attn_w_c,
           L0_ffn_gate_w, L0_ffn_gate_s, L0_ffn_up_w, L0_ffn_up_s,
           L0_ffn_down_w, L0_ffn_down_s):
    x = np.ascontiguousarray(np.asarray(L0_attn_hidden_states, dtype=np.float32))
    w_qkv = np.ascontiguousarray(np.asarray(L0_attn_w_qkv, dtype=np.float32))
    w_c = np.ascontiguousarray(np.asarray(L0_attn_w_c, dtype=np.float32))

    def i8view(w):
        w = np.ascontiguousarray(np.asarray(w, dtype=np.int32))
        return np.ascontiguousarray(w.view(np.int8)[:, 0::4])

    gw8 = np.ascontiguousarray(i8view(L0_ffn_gate_w).T)
    uw8 = np.ascontiguousarray(i8view(L0_ffn_up_w).T)
    dw8 = np.ascontiguousarray(i8view(L0_ffn_down_w).T)

    scal = np.empty((128, 3), np.float32)
    scal[:, 0] = np.asarray(L0_ffn_gate_s, np.float32).reshape(())
    scal[:, 1] = np.asarray(L0_ffn_up_s, np.float32).reshape(())
    scal[:, 2] = np.asarray(L0_ffn_down_s, np.float32).reshape(())

    ii = np.arange(128, dtype=np.float64)
    r2l = np.empty((5, H * 128), np.float64)
    for h in range(H):
        A = float(SLOPES[h]) / SCALE
        u = A * ii
        u_hi = u.astype(ml_dtypes.bfloat16).astype(np.float64)
        u_lo = u - u_hi
        A_hi = np.float64(np.asarray(A, dtype=ml_dtypes.bfloat16).astype(np.float64))
        A_lo = A - A_hi
        r2l[0, h * 128:(h + 1) * 128] = u_hi
        r2l[1, h * 128:(h + 1) * 128] = u_lo
        r2l[2, h * 128:(h + 1) * 128] = A_hi
        r2l[3, h * 128:(h + 1) * 128] = A_hi
        r2l[4, h * 128:(h + 1) * 128] = A_lo
    r2l = np.concatenate([r2l, np.zeros((123, H * 128))]).astype(ml_dtypes.bfloat16)
    jj = np.arange(512, dtype=np.float64)
    jh = 4.0 * np.floor(jj / 4.0)
    jl = jj - jh
    r2r = np.concatenate([np.stack([np.ones(512), np.ones(512), -jh, -jl, -jh]),
                          np.zeros((123, 512))]).astype(ml_dtypes.bfloat16)
    tri = np.where(np.arange(128)[None, :] >= np.arange(128)[:, None],
                   0.0, NEG).astype(np.float32)
    bias_tab = np.zeros((128, H * 80), np.float32)
    for h in range(H):
        for w in range(2):
            for kt in range(40):
                bias_tab[:, h * 80 + w * 40 + kt] = (
                    float(SLOPES[h]) * (128.0 * kt - 4096.0 - 512.0 * w))
    ident = np.eye(128, dtype=ml_dtypes.bfloat16)

    in_maps = []
    for c in range(8):
        b, p = c // 4, c % 4
        b0 = 1024 * p
        xctx = np.zeros((CTX, E), np.float32)
        lo_abs = b0 - 4096
        nreal = b0 + 1024 - max(0, lo_abs)
        xctx[CTX - nreal:] = x[b, max(0, lo_abs):b0 + 1024]
        npad = 32 - 8 * p
        vones = np.zeros((128, CTXT), np.float32)
        vones[:, npad:] = 1.0
        in_maps.append(dict(
            x_ctx=xctx, w_qkv=w_qkv, w_c=w_c, gw8=gw8, uw8=uw8, dw8=dw8,
            scal=scal, aux_r2l=r2l, aux_r2r=r2r, aux_vones=vones,
            aux_bias=bias_tab, aux_tri=tri, aux_id=ident))

    if not _NC_CACHE:
        _NC_CACHE.append(build_nc())
    nc = _NC_CACHE[0]
    res = run_bass_kernel_spmd(nc, in_maps, core_ids=list(range(8)))

    outp = np.empty((B, S, E), np.float32)
    for c in range(8):
        b, p = c // 4, c % 4
        outp[b, 1024 * p:1024 * (p + 1)] = res.results[c]["out"]
    return outp


# version 14 (final)
# speedup vs baseline: 1.0905x; 1.0545x over previous
import math
from contextlib import ExitStack

import numpy as np
import ml_dtypes

import concourse.bass as bass
import concourse.mybir as mybir
from concourse.bass_utils import run_bass_kernel_spmd

F32 = mybir.dt.float32
BF16 = mybir.dt.bfloat16
I8 = mybir.dt.int8
AF = mybir.ActivationFunctionType
ALU = mybir.AluOpType

B, S, E, H, Dh = 2, 4096, 4096, 32, 128
FFI = 11008
SCALE = 1.0 / math.sqrt(128.0)
TOK = 1024
CTX = 5120
CTXT = 40
DT = 32
ITT = 86
EPS = 1e-5
NEG = -1.0e30

IC_SIZES = [15, 15, 14, 14, 14, 14]
IC_BASE = [0, 15, 30, 44, 58, 72]
DEBUG_DUMP = False
WCG = 16
DEG = 8


def _alibi_slopes(n: int) -> np.ndarray:
    cp2 = 2 ** int(math.floor(math.log2(n)))
    base = 2.0 ** (-(2.0 ** (-(math.log2(cp2) - 3))))
    slopes = base ** np.arange(1, cp2 + 1, dtype=np.float64)
    if cp2 != n:
        extra_base = 2.0 ** (-(2.0 ** (-(math.log2(2 * cp2) - 3))))
        rem = min(cp2, n - cp2)
        slopes = np.concatenate(
            [slopes, extra_base ** np.arange(1, 1 + 2 * rem, 2, dtype=np.float64)])
    return slopes.astype(np.float32)


SLOPES = _alibi_slopes(H)


def _lo(h, w):
    lo = 0
    for kt in range(36 + 4 * w):
        if SLOPES[h] * (128 * (kt + 1) - 1 - 4096 - 512 * w) < -150.0:
            lo = kt + 1
    return lo


LO = [[_lo(h, w) for w in range(2)] for h in range(H)]
for h in range(H):
    for w in range(2):
        assert LO[h][w] <= 32 + 4 * w


class Sem:
    def __init__(self, nc, st, name):
        self.h = st.enter_context(nc.semaphore(name))
        self.n = 0
        self.last_wait = {}

    def inc(self, instr, k=1):
        instr.then_inc(self.h, k)
        self.n += k
        return instr

    def wait(self, eng, val=None):
        v = self.n if val is None else val
        q = type(eng).__name__
        if self.last_wait.get(q, -1) >= v:
            return
        self.last_wait[q] = v
        eng.wait_ge(self.h, v)


def build_nc():
    nc = bass.Bass("TRN2", target_bir_lowering=False)

    x_ctx = nc.dram_tensor("x_ctx", [CTX, E], F32, kind="ExternalInput")
    w_qkv = nc.dram_tensor("w_qkv", [E, H * Dh + 2 * Dh], F32, kind="ExternalInput")
    w_c = nc.dram_tensor("w_c", [E, E], F32, kind="ExternalInput")
    gw8 = nc.dram_tensor("gw8", [E, FFI], I8, kind="ExternalInput")
    uw8 = nc.dram_tensor("uw8", [E, FFI], I8, kind="ExternalInput")
    dw8 = nc.dram_tensor("dw8", [FFI, E], I8, kind="ExternalInput")
    scal = nc.dram_tensor("scal", [128, 3], F32, kind="ExternalInput")
    aux_r2l = nc.dram_tensor("aux_r2l", [128, H * 128], BF16, kind="ExternalInput")
    aux_r2r = nc.dram_tensor("aux_r2r", [128, 1024], BF16, kind="ExternalInput")
    aux_bias = nc.dram_tensor("aux_bias", [128, H * 80], F32, kind="ExternalInput")
    aux_tri = nc.dram_tensor("aux_tri", [128, 128], F32, kind="ExternalInput")
    aux_id = nc.dram_tensor("aux_id", [128, 128], BF16, kind="ExternalInput")

    out = nc.dram_tensor("out", [TOK, E], F32, kind="ExternalOutput")
    if DEBUG_DUMP:
        dbg_xn = nc.dram_tensor("dbg_xn", [CTX, E], BF16, kind="ExternalOutput")
        dbg_at = nc.dram_tensor("dbg_at", [TOK, E], BF16, kind="ExternalOutput")
        dbg_k = nc.dram_tensor("dbg_k", [128, CTX], BF16, kind="ExternalOutput")
        dbg_m = nc.dram_tensor("dbg_m", [128, 15 * 1024], BF16, kind="ExternalOutput")

    xn_sp = nc.dram_tensor("xn_sp", [CTX, E], BF16, kind="Internal")
    attn_sp = nc.dram_tensor("attn_sp", [E, TOK], BF16, kind="Internal")
    gsc = nc.dram_tensor("gsc", [E, FFI], BF16, kind="Internal")
    usc = nc.dram_tensor("usc", [E, FFI], BF16, kind="Internal")
    dsc = nc.dram_tensor("dsc", [FFI, E], BF16, kind="Internal")

    st = ExitStack()
    e = st.enter_context

    sb_scal = e(nc.sbuf_tensor("sb_scal", [128, 3], F32))
    sb_usds = e(nc.sbuf_tensor("sb_usds", [128, 1], F32))
    sb_tri = e(nc.sbuf_tensor("sb_tri", [128, 128], F32))
    sb_id = e(nc.sbuf_tensor("sb_id", [128, 128], BF16))
    ones_sb = e(nc.sbuf_tensor("ones_sb", [128, 128], BF16))
    xn_own = e(nc.sbuf_tensor("xn_own", [128, DT * 1024], BF16))

    d_aux = Sem(nc, st, "d_aux")
    d_x = [Sem(nc, st, f"d_x{i}") for i in range(2)]
    d_xnw = [Sem(nc, st, f"d_xnw{i}") for i in range(10)]
    d_xt = [Sem(nc, st, f"d_xt{i}") for i in range(2)]
    d_wkv = Sem(nc, st, "d_wkv")
    d_gsc = Sem(nc, st, "d_gsc")
    d_wq = [Sem(nc, st, f"d_wq{i}") for i in range(2)]
    d_as = [Sem(nc, st, f"d_as{i}") for i in range(2)]
    d_at = Sem(nc, st, "d_at")
    d_wcq = [Sem(nc, st, f"d_wcq{i}") for i in range(2)]
    d_xo = [Sem(nc, st, f"d_xo{i}") for i in range(2)]
    d_out = [Sem(nc, st, f"d_out{i}") for i in range(8)]
    d_outa = [Sem(nc, st, f"d_outa{i}") for i in range(8)]
    d_xn2 = Sem(nc, st, "d_xn2")
    d_wg = [Sem(nc, st, f"d_wg{i}") for i in range(2)]
    d_dw = [Sem(nc, st, f"d_dw{i}") for i in range(2)]

    c_ln_v = Sem(nc, st, "c_ln_v")
    c_ln_s = Sem(nc, st, "c_ln_s")
    c_kv_t = Sem(nc, st, "c_kv_t")
    c_kv_e = Sem(nc, st, "c_kv_e")
    c_q = Sem(nc, st, "c_q")
    c_qe = Sem(nc, st, "c_qe")
    c_sc = Sem(nc, st, "c_sc")
    c_mk = Sem(nc, st, "c_mk")
    c_ex = Sem(nc, st, "c_ex")
    c_pv = Sem(nc, st, "c_pv")
    c_pve = Sem(nc, st, "c_pve")
    c_wc = Sem(nc, st, "c_wc")
    c_wce = Sem(nc, st, "c_wce")
    c_gu = Sem(nc, st, "c_gu")
    c_sl = Sem(nc, st, "c_sl")
    c_m = Sem(nc, st, "c_m")
    c_dn = Sem(nc, st, "c_dn")
    c_dne = Sem(nc, st, "c_dne")
    c_misc = Sem(nc, st, "c_misc")

    P01 = e(nc.psum_tensor("P01", [128, 1024], F32))
    P23 = e(nc.psum_tensor("P23", [128, 1024], F32))
    P45 = e(nc.psum_tensor("P45", [128, 1024], F32))
    P67 = e(nc.psum_tensor("P67", [128, 1024], F32))

    blk = st.enter_context(nc.Block())

    kv_scope = ExitStack()
    ke = kv_scope.enter_context
    K_sb = ke(nc.sbuf_tensor("K_sb", [128, CTX], BF16, side="right"))
    V_sb = ke(nc.sbuf_tensor("V_sb", [128, CTXT * 128], BF16, side="right"))

    ln_scope = ExitStack()
    le = ln_scope.enter_context
    xbuf = [le(nc.sbuf_tensor(f"xbuf{i}", [128, E], F32, side="right")) for i in range(2)]
    xnbuf = [le(nc.sbuf_tensor("xnbuf0", [128, E], BF16, side="right"))]
    stt = le(nc.sbuf_tensor("stt", [128, 48], F32, side="right"))
    mv = le(nc.sbuf_tensor("mv", [128, 2], F32, side="right"))
    veps = le(nc.sbuf_tensor("veps", [128, 1], F32, side="right"))
    lnl = le(nc.sbuf_tensor("lnl", [128, 1], F32, side="right"))
    rstd = le(nc.sbuf_tensor("rstd", [128, 1], F32, side="right"))
    nbias = le(nc.sbuf_tensor("nbias", [128, 1], F32, side="right"))

    wkv_scope = ExitStack()
    wkv_sb = wkv_scope.enter_context(nc.sbuf_tensor("wkv_sb", [128, DT * 256], BF16, side="right"))
    xt_scope = ExitStack()
    xt_sl = [xt_scope.enter_context(
        nc.sbuf_tensor("xt_sl0", [128, DT * 512], BF16, side="right"))]

    cast_jobs = []
    for r in range(DT):
        cast_jobs.append((gsc.ap()[r * 128:(r + 1) * 128, :],
                          gw8.ap()[r * 128:(r + 1) * 128, :]))
    for r in range(DT):
        cast_jobs.append((usc.ap()[r * 128:(r + 1) * 128, :],
                          uw8.ap()[r * 128:(r + 1) * 128, :]))
    for r in range(ITT):
        cast_jobs.append((dsc.ap()[r * 128:(r + 1) * 128, :],
                          dw8.ap()[r * 128:(r + 1) * 128, :]))
    cast_pos = [0]

    def emit_casts(g, k):
        for _ in range(k):
            if cast_pos[0] < len(cast_jobs):
                dst, src = cast_jobs[cast_pos[0]]
                d_gsc.inc(g.dma_start(dst, src), 16)
                cast_pos[0] += 1

    att_scope = ExitStack()
    ae = att_scope.enter_context
    wq_sl = [ae(nc.sbuf_tensor(f"wq_sl{i}", [128, DT * 128], BF16, side="right")) for i in range(2)]
    q_sl = [ae(nc.sbuf_tensor(f"q_sl{i}", [128, 1024], BF16, side="right")) for i in range(2)]
    probs = [ae(nc.sbuf_tensor(f"probs{i}", [128, 1024], BF16, side="right")) for i in range(3)]
    recd_sb = ae(nc.sbuf_tensor("recd_sb", [128, 512], F32, side="right"))
    astage = [ae(nc.sbuf_tensor(f"astage{i}", [128, 512], BF16, side="right")) for i in range(2)]
    r2l_sb = ae(nc.sbuf_tensor("r2l_sb", [128, 2 * 128], BF16, side="right"))
    r2r_sb = ae(nc.sbuf_tensor("r2r_sb", [128, 1024], BF16, side="right"))
    bias_sb = ae(nc.sbuf_tensor("bias_sb", [128, 2 * 80], F32, side="right"))

    def em(s):
        d_aux.inc(s.dma_start(sb_tri[:], aux_tri.ap()), 16)
        d_aux.inc(s.dma_start(sb_id[:], aux_id.ap()), 16)
        d_aux.inc(s.dma_start(sb_scal[:], scal.ap()), 16)
        d_aux.inc(s.dma_start(r2r_sb[:], aux_r2r.ap()), 16)
    blk.sync(em)

    def em(g):
        d_wkv.inc(g.dma_start(
            wkv_sb[:].rearrange("p (t c) -> p t c", t=DT),
            w_qkv.ap()[:, H * Dh:H * Dh + 256].rearrange("(t p) c -> p t c", p=128)),
            16)
        for h in range(2):
            d_wq[h].inc(g.dma_start(
                wq_sl[h][:].rearrange("p (t c) -> p t c", t=DT),
                w_qkv.ap()[:, h * 128:(h + 1) * 128]
                .rearrange("(t p) c -> p t c", p=128)), 16)
            d_wq[h].inc(g.dma_start(
                bias_sb[:, h * 80:(h + 1) * 80],
                aux_bias.ap()[:, h * 80:(h + 1) * 80]), 16)
            d_wq[h].inc(g.dma_start(
                r2l_sb[:, h * 128:(h + 1) * 128],
                aux_r2l.ap()[:, h * 128:(h + 1) * 128]), 16)
    blk.gpsimd(em)

    def em(v):
        d_aux.wait(v)
        c_misc.inc(v.tensor_tensor(sb_usds[:], sb_scal[:, 1:2], sb_scal[:, 2:3],
                                   ALU.mult))
        c_misc.inc(v.memset(ones_sb[:], 1.0))
    blk.vector(em)

    ln_done = {}
    kv_t_done = {}
    kv_e_done = {}
    k_ready = {}
    v_ready = {}
    ln_seq = []
    kv_seq = []

    KP = P67[:, 512:1024]
    VP = P67[:, 512:640]

    def emit_ln(t):
        sl = len(ln_seq) % 2
        prev2 = ln_seq[-2] if len(ln_seq) >= 2 else None
        prev1 = ln_seq[-1] if len(ln_seq) >= 1 else None
        ln_seq.append(t)

        def em(g, t=t, sl=sl, prev2=prev2):
            if prev2 is not None:
                c_ln_v.wait(g, ln_done[prev2])
            d_x[sl].inc(g.dma_start(xbuf[sl][:],
                                    x_ctx.ap()[t * 128:(t + 1) * 128, :]), 16)
        blk.gpsimd(em)

        def em(v, t=t, sl=sl):
            d_x[sl].wait(v)
            for k in range(8):
                c_ln_v.inc(v.bn_stats(stt[:, 6 * k:6 * k + 6],
                                      xbuf[sl][:, 512 * k:512 * (k + 1)]))
            c_ln_v.wait(v)
            c_ln_v.inc(v.bn_aggr(mv[:], stt[:]))
            c_ln_v.wait(v)
            c_ln_v.inc(v.tensor_scalar_add(veps[:], mv[:, 1:2], EPS))
        blk.vector(em)

        def em(sc):
            c_ln_v.wait(sc)
            c_ln_s.inc(sc.activation(lnl[:], veps[:], AF.Ln))
            c_ln_s.wait(sc)
            c_ln_s.inc(sc.activation(rstd[:], lnl[:], AF.Exp, scale=-0.5))
        blk.scalar(em)

        def em(v):
            c_ln_s.wait(v)
            c_ln_v.inc(v.tensor_scalar(nbias[:], mv[:, 0:1], rstd[:], -1.0,
                                       ALU.mult, ALU.mult))
        blk.vector(em)

        def em(v, t=t, sl=sl, prev1=prev1):
            c_ln_v.wait(v)
            if prev1 is not None:
                d_xnw[prev1 // 4].wait(v, 16 * ((prev1 % 4) + 1))
            c_ln_v.inc(v.tensor_scalar(xnbuf[0][:], xbuf[sl][:], rstd[:],
                                       nbias[:], ALU.mult, ALU.add))
            ln_done[t] = c_ln_v.n
        blk.vector(em)

        def em(sc, t=t, sl=sl):
            c_ln_v.wait(sc)
            d_xnw[t // 4].inc(
                sc.dma_start(xn_sp.ap()[t * 128:(t + 1) * 128, :], xnbuf[0][:]), 16)
        blk.scalar(em)

        def em(g):
            emit_casts(g, 2)
        blk.gpsimd(em)

    def emit_kv(tc):
        xt = xt_sl[0]
        prev = kv_seq[-1] if kv_seq else None
        kv_seq.append(tc)

        def em(s, tc=tc, xt=xt, prev=prev):
            d_xnw[tc].wait(s, 16 * 4)
            if prev is not None:
                c_kv_t.wait(s, kv_t_done[prev])
            for dt in range(DT):
                d_xt[0].inc(s.dma_start(
                    xt[:, dt * 512:dt * 512 + 512],
                    xn_sp.ap()[tc * 512:(tc + 1) * 512, dt * 128:(dt + 1) * 128],
                    transpose=True), 16)
        blk.sync(em)

        def em(t, tc=tc, xt=xt, prev=prev):
            d_xt[0].wait(t)
            d_wkv.wait(t)
            if tc == kv_seq[0]:
                c_misc.wait(t)
            if prev is not None:
                c_kv_e.wait(t, k_ready[prev])
            for dt in range(DT):
                c_kv_t.inc(t.matmul(
                    KP, wkv_sb[:, dt * 256:dt * 256 + 128],
                    xt[:, dt * 512:dt * 512 + 512],
                    start=(dt == 0), stop=(dt == DT - 1)))
        blk.tensor(em)

        def em(v, tc=tc):
            c_kv_t.wait(v)
            c_kv_e.inc(v.tensor_copy(K_sb[:, tc * 512:(tc + 1) * 512], KP))
            k_ready[tc] = c_kv_e.n
        blk.vector(em)

        for j in range(4):
            def emt(t, tc=tc, j=j):
                c_kv_e.wait(t)
                for dt in range(DT):
                    c_kv_t.inc(t.matmul(
                        VP, xt[:, dt * 512 + j * 128:dt * 512 + (j + 1) * 128],
                        wkv_sb[:, dt * 256 + 128:dt * 256 + 256],
                        start=(dt == 0), stop=(dt == DT - 1)))
            blk.tensor(emt)

            def emv(v, tc=tc, j=j):
                c_kv_t.wait(v)
                c_misc.wait(v, 2)
                c_kv_e.inc(v.tensor_copy(
                    V_sb[:, (4 * tc + j) * 128:(4 * tc + j) * 128 + 128], VP))
            blk.vector(emv)
        kv_t_done[tc] = c_kv_t.n
        kv_e_done[tc] = c_kv_e.n
        v_ready[tc] = c_kv_e.n

        def em(g):
            emit_casts(g, 4)
        blk.gpsimd(em)

    HEADS_AT = {tc: [] for tc in range(10)}
    for h in range(H):
        HEADS_AT[max(0, LO[h][0] // 4)].append(h)

    PHASE_A_PLAN = [True]

    def emit_phase_a():
        for tc in range(9, -1, -1):
            for t in range(4 * tc, 4 * tc + 4):
                emit_ln(t)
            emit_kv(tc)
            if tc == 8:
                def em(s):
                    for q in (8, 9):
                        d_xnw[q].wait(s, 16 * 4)
                    for dt in range(DT):
                        for hf in range(2):
                            d_xn2.inc(s.dma_start(
                                xn_own[:, dt * 1024 + hf * 512:
                                       dt * 1024 + hf * 512 + 512],
                                xn_sp.ap()[4096 + hf * 512:
                                           4096 + hf * 512 + 512,
                                           dt * 128:(dt + 1) * 128],
                                transpose=True), 16)
                blk.sync(em)
            for h in HEADS_AT.get(tc, []):
                emit_head(h)


    sc_free = {}
    wq_ex_used = {}
    pv_free = {}
    probs_free = {}
    as_used = {}
    wq_used = {}
    qe_done = {}
    qsl_free = {}

    HALF_PS = [P01[:, 0:512], P01[:, 512:1024], P45[:, 0:512], P45[:, 512:1024]]
    PAIR_PS = [P01, P45]
    AT_PS = P23[:, 0:512]
    DN_PS = P23[:, 512:1024]
    Q_PS = [P67[:, 0:512], P67[:, 0:512]]
    as_ctr = [0]

    def emit_head(h):
        wsl = h % 2

        def em(g, h=h, wsl=wsl):
            if h >= 2:
                c_q.wait(g, wq_used[wsl])
                c_ex.wait(g, wq_ex_used.get(wsl, 0))
                d_wq[wsl].inc(g.dma_start(
                    wq_sl[wsl][:].rearrange("p (t c) -> p t c", t=DT),
                    w_qkv.ap()[:, h * 128:(h + 1) * 128]
                    .rearrange("(t p) c -> p t c", p=128)), 16)
                d_wq[wsl].inc(g.dma_start(
                    bias_sb[:, wsl * 80:(wsl + 1) * 80],
                    aux_bias.ap()[:, h * 80:(h + 1) * 80]), 16)
                d_wq[wsl].inc(g.dma_start(
                    r2l_sb[:, wsl * 128:(wsl + 1) * 128],
                    aux_r2l.ap()[:, h * 128:(h + 1) * 128]), 16)
            emit_casts(g, 3)
        blk.gpsimd(em)

        for half in range(2):
            def em(t, h=h, wsl=wsl, half=half):
                d_wq[wsl].wait(t, 48 * (h // 2 + 1))
                if h == 0 and half == 0:
                    c_misc.wait(t)
                    d_aux.wait(t)
                    d_xn2.wait(t)
                if half == 1:
                    c_qe.wait(t, qe_done[(h, 0)])
                elif h >= 1:
                    c_qe.wait(t, qe_done[(h - 1, 1)])
                    c_ex.wait(t)
                for dt in range(DT):
                    c_q.inc(t.matmul(
                        Q_PS[half],
                        wq_sl[wsl][:, dt * 128:(dt + 1) * 128],
                        xn_own[:, dt * 1024 + half * 512:dt * 1024 + half * 512 + 512],
                        start=(dt == 0), stop=(dt == DT - 1)))
                if half == 1:
                    wq_used[wsl] = c_q.n
            blk.tensor(em)

            def em(v, h=h, half=half):
                c_q.wait(v)
                if h >= 2 and half == 0:
                    c_sc.wait(v, qsl_free[h % 2])
                c_qe.inc(v.tensor_copy(q_sl[h % 2][:, half * 512:half * 512 + 512],
                                       Q_PS[half]))
                qe_done[(h, half)] = c_qe.n
            blk.vector(em)

        def units_of(w):
            lo, hi = LO[h][w], 36 + 4 * w
            ndiag = 32 + 4 * w
            us = [("S", (kt,)) for kt in range(hi - 1, max(ndiag, lo) - 1, -1)]
            kt = ndiag - 1
            while kt >= lo + 1:
                us.append(("P", (kt, kt - 1)))
                kt -= 2
            if kt == lo:
                us.append(("S", (kt,)))
            return us

        ex_at = {}
        nx = c_ex.n
        for w in range(2):
            for u in units_of(w):
                nx += 1
                for kt in u[1]:
                    ex_at[(w, kt)] = nx

        for w in range(2):
            lo, hi = LO[h][w], 36 + 4 * w
            ndiag = 32 + 4 * w
            units = units_of(w)
            sc_at = {}
            mk_at = {}
            pv_at = {}
            uslot = {}
            nsingle = [0]
            npair = [0]

            def emt(t, h=h, w=w, units=units, lo=lo, hi=hi, ndiag=ndiag):
                def scores(ui):
                    kind, ukts = units[ui]
                    if kind == "S":
                        ssl = nsingle[0] % 4
                        nsingle[0] += 1
                        uslot[ui] = ("S", ssl)
                        kt = ukts[0]
                        js = kt - ndiag
                        coloff = 128 * js if js > 0 else 0
                        sps = HALF_PS[ssl]
                        if ("h", ssl) in sc_free:
                            c_ex.wait(t, sc_free[("h", ssl)])
                        c_kv_e.wait(t, k_ready[kt // 4])
                        if w == 0 and ui == 0:
                            c_qe.wait(t, qe_done[(h, 1)])
                        t.matmul(sps[:, coloff:512],
                                 K_sb[:, kt * 128:(kt + 1) * 128],
                                 q_sl[h % 2][:, 512 * w + coloff:512 * (w + 1)],
                                 start=True, stop=False)
                        c_sc.inc(t.matmul(
                            sps[:, coloff:512],
                            r2l_sb[:, (h % 2) * 128:(h % 2 + 1) * 128],
                            r2r_sb[:, coloff:512],
                            start=False, stop=True))
                        sc_at[kt] = c_sc.n
                        sc_free[("h", ssl)] = ex_at[(w, kt)]
                    else:
                        psl = npair[0] % 2
                        npair[0] += 1
                        uslot[ui] = ("P", psl)
                        pps = PAIR_PS[psl]
                        for huf, kt in enumerate(ukts):
                            ssl = 2 * psl + huf
                            if ("h", ssl) in sc_free:
                                c_ex.wait(t, sc_free[("h", ssl)])
                            c_kv_e.wait(t, k_ready[kt // 4])
                            t.matmul(pps[:, huf * 512:huf * 512 + 512],
                                     K_sb[:, kt * 128:(kt + 1) * 128],
                                     q_sl[h % 2][:, 512 * w:512 * (w + 1)],
                                     start=True, stop=False)
                            c_sc.inc(t.matmul(
                                pps[:, huf * 512:huf * 512 + 512],
                                r2l_sb[:, (h % 2) * 128:(h % 2 + 1) * 128],
                                r2r_sb[:, huf * 512:huf * 512 + 512],
                                start=False, stop=True))
                            sc_at[kt] = c_sc.n
                            sc_free[("h", ssl)] = ex_at[(w, kt)]

                def pv(ui):
                    kind, ukts = units[ui]
                    psl = ui % 3
                    c_ex.wait(t, ex_at[(w, ukts[0])])
                    for huf, kt in enumerate(ukts):
                        js = kt - ndiag
                        coloff = 128 * js if js > 0 else 0
                        c_kv_e.wait(t, v_ready[kt // 4])
                        first = (kt == hi - 1)
                        if first:
                            coloff = 0
                            if ("at", 0) in pv_free:
                                c_pve.wait(t, pv_free[("at", 0)])
                        pr = probs[psl][:, huf * 512 + coloff:huf * 512 + 512]
                        c_pv.inc(t.matmul(
                            AT_PS[:, coloff:512],
                            V_sb[:, kt * 128:(kt + 1) * 128], pr,
                            start=first, stop=(kt == lo)))
                        c_pv.inc(t.matmul(
                            DN_PS[:, coloff:512],
                            ones_sb[:], pr,
                            start=first, stop=(kt == lo)))
                        pv_at[kt] = c_pv.n

                for ui in range(len(units)):
                    scores(ui)
                    if ui >= 2:
                        pv(ui - 2)
                for ui in (len(units) - 2, len(units) - 1):
                    if ui >= 0:
                        pv(ui)
            blk.tensor(emt)

            def emv(v, w=w, units=units, ndiag=ndiag):
                for ui, (kind, ukts) in enumerate(units):
                    if kind != "S" or ukts[0] < ndiag:
                        continue
                    kt = ukts[0]
                    coloff = 128 * (kt - ndiag)
                    sps = HALF_PS[uslot[ui][1]]
                    c_sc.wait(v, sc_at[kt])
                    c_mk.inc(v.tensor_tensor(
                        sps[:, coloff:coloff + 128],
                        sps[:, coloff:coloff + 128], sb_tri[:], ALU.add))
                    mk_at[kt] = c_mk.n
            blk.vector(emv)

            def ems(sc, h=h, w=w, units=units, ndiag=ndiag):
                for ui, (kind, ukts) in enumerate(units):
                    psl = ui % 3
                    kt = ukts[0]
                    if ("pb", psl) in probs_free:
                        c_pv.wait(sc, probs_free[("pb", psl)])
                    if ui >= 3:
                        c_pv.wait(sc, pv_at.get(units[ui - 3][1][-1], 0))
                    c_sc.wait(sc, sc_at[ukts[-1]])
                    if kind == "S" and kt >= ndiag:
                        c_mk.wait(sc, mk_at[kt])
                    d_wq[h % 2].wait(sc, 48 * (h // 2 + 1))
                    bias_ap = bias_sb[:, (h % 2) * 80 + w * 40 + kt:
                                      (h % 2) * 80 + w * 40 + kt + 1]
                    if kind == "S":
                        js = kt - ndiag
                        coloff = 128 * js if js > 0 else 0
                        sps = HALF_PS[uslot[ui][1]]
                        if ui == 0 and coloff > 0:
                            sc.memzero(probs[psl][:, 0:coloff])
                        c_ex.inc(sc.activation(
                            probs[psl][:, coloff:512], sps[:, coloff:512],
                            AF.Exp, bias=bias_ap, scale=SCALE))
                    else:
                        pps = PAIR_PS[uslot[ui][1]]
                        c_ex.inc(sc.activation(
                            probs[psl][:], pps[:], AF.Exp,
                            bias=bias_ap, scale=SCALE))
                    assert c_ex.n == ex_at[(w, kt)]
            blk.scalar(ems)
            for ui, (kind, ukts) in enumerate(units):
                probs_free[("pb", ui % 3)] = pv_at[ukts[-1]]
            if w == 1:
                qsl_free[h % 2] = c_sc.n
                wq_ex_used[h % 2] = c_ex.n
            pv_after = c_pv.n

            asl = as_ctr[0] % 2
            as_ctr[0] += 1

            def em(v, asl=asl, pa=pv_after):
                c_pv.wait(v, pa)
                if ("as", asl) in as_used:
                    d_as[asl].wait(v, as_used[("as", asl)])
                c_pve.inc(v.reciprocal(recd_sb[:], DN_PS))
                c_pve.wait(v)
                c_pve.inc(v.tensor_tensor(astage[asl][:], AT_PS, recd_sb[:],
                                          ALU.mult))
            blk.vector(em)
            pv_free[("at", 0)] = c_pve.n
            pve_after = c_pve.n

            def em(g, asl=asl, h=h, w=w, pa=pve_after):
                c_pve.wait(g, pa)
                d_as[asl].inc(g.dma_start(
                    attn_sp.ap()[h * 128:(h + 1) * 128, w * 512:(w + 1) * 512],
                    astage[asl][:]), 16)
            blk.gpsimd(em)
            as_used[("as", asl)] = d_as[asl].n

    emit_phase_a()

    att_scope.close()
    xt_scope.close()
    wkv_scope.close()
    ln_scope.close()
    kv_scope.close()

    wc_scope = ExitStack()
    we = wc_scope.enter_context
    attnT = we(nc.sbuf_tensor("attnT", [128, DT * 1024], BF16, side="right"))
    wcq_sl = [we(nc.sbuf_tensor(f"wcq_sl{i}", [128, DT * 256], BF16, side="right"))
              for i in range(2)]
    xo_sl = [we(nc.sbuf_tensor(f"xo_sl{i}", [128, 256], F32, side="right")) for i in range(2)]
    wstage = [we(nc.sbuf_tensor(f"wstage{i}", [128, 256], F32, side="right")) for i in range(2)]

    def em(s):
        for i in range(2):
            d_as[i].wait(s, d_as[i].n)
        c_sc.wait(s)
        c_ex.wait(s)
        c_pv.wait(s)
        c_pve.wait(s)
        c_q.wait(s)
        c_qe.wait(s)
        c_ln_v.wait(s)
        c_kv_t.wait(s)
        c_kv_e.wait(s)
        c_ln_s.wait(s)
        d_at.inc(s.dma_start(
            attnT[:].rearrange("p (t c) -> p t c", t=DT),
            attn_sp.ap().rearrange("(t p) c -> p t c", p=128)), 16)
    blk.sync(em)

    wcq_used = {}
    xo_used = {}
    wst_used = {}
    WC_PS = [P01[:, 0:256], P01[:, 512:768], P23[:, 0:256], P23[:, 512:768],
             P45[:, 0:256], P45[:, 512:768], P67[:, 0:256], P67[:, 512:768]]
    wc_bank_used = {}

    for eg in range(WCG):
        qsl = eg % 2

        def em(g, eg=eg, qsl=qsl):
            if eg >= 2:
                c_wc.wait(g, wcq_used[qsl])
            else:
                c_sc.wait(g)
                c_pv.wait(g)
                c_pve.wait(g)
                c_ex.wait(g)
            d_wcq[qsl].inc(g.dma_start(
                wcq_sl[qsl][:].rearrange("p (t c) -> p t c", t=DT),
                w_c.ap()[:, eg * 256:(eg + 1) * 256]
                .rearrange("(t p) c -> p t c", p=128)), 16)
        blk.gpsimd(em)

        for tt in range(8):
            bank = (eg * 8 + tt) % 8
            xsl = (eg * 8 + tt) % 2
            wsl2 = (eg * 8 + tt) % 2

            def em(s, eg=eg, tt=tt, xsl=xsl):
                if ("xo", xsl) in xo_used:
                    c_wce.wait(s, xo_used[("xo", xsl)])
                d_xo[xsl].inc(s.dma_start(
                    xo_sl[xsl][:],
                    x_ctx.ap()[4096 + tt * 128:4096 + (tt + 1) * 128,
                               eg * 256:(eg + 1) * 256]), 16)
            blk.sync(em)

            def em(t, eg=eg, tt=tt, qsl=qsl, bank=bank):
                d_at.wait(t)
                d_wcq[qsl].wait(t, 16 * (eg // 2 + 1))
                if ("wb", bank) in wc_bank_used:
                    c_wce.wait(t, wc_bank_used[("wb", bank)])
                for ein in range(DT):
                    c_wc.inc(t.matmul(
                        WC_PS[bank],
                        attnT[:, ein * 1024 + tt * 128:ein * 1024 + (tt + 1) * 128],
                        wcq_sl[qsl][:, ein * 256:(ein + 1) * 256],
                        start=(ein == 0), stop=(ein == DT - 1)))
            blk.tensor(em)
            wcq_used[qsl] = c_wc.n
            wc_after = c_wc.n

            def em(v, bank=bank, xsl=xsl, wsl2=wsl2, wa=wc_after):
                c_wc.wait(v, wa)
                d_xo[xsl].wait(v)
                if ("ws", wsl2) in wst_used:
                    tt2, n2 = wst_used[("ws", wsl2)]
                    d_out[tt2].wait(v, n2)
                c_wce.inc(v.scalar_tensor_tensor(
                    wstage[wsl2][:], WC_PS[bank], 1.0, xo_sl[xsl][:],
                    ALU.mult, ALU.add))
            blk.vector(em)
            wc_bank_used[("wb", bank)] = c_wce.n
            xo_used[("xo", xsl)] = c_wce.n
            wce_after = c_wce.n

            def em(s, eg=eg, tt=tt, wsl2=wsl2, wa=wce_after):
                c_wce.wait(s, wa)
                d_out[tt].inc(s.dma_start(
                    out.ap()[tt * 128:(tt + 1) * 128, eg * 256:(eg + 1) * 256],
                    wstage[wsl2][:]), 16)
            blk.sync(em)
            wst_used[("ws", wsl2)] = (tt, d_out[tt].n)

    wc_scope.close()
    wc_out_base = [d_out[tt].n for tt in range(8)]

    ffn_scope = ExitStack()
    fe = ffn_scope.enter_context
    m_sb = fe(nc.sbuf_tensor("m_sb", [128, 15 * 1024], BF16, side="right"))
    wg_sl = [fe(nc.sbuf_tensor(f"wg_sl{i}", [128, DT * 512], BF16, side="right"))
             for i in range(2)]
    dwt_sl = [fe(nc.sbuf_tensor(f"dwt_sl{i}", [128, 15 * 512], BF16, side="right"))
              for i in range(2)]
    silu_st = [fe(nc.sbuf_tensor(f"silu_st{i}", [128, 1024], F32, side="right"))
               for i in range(2)]
    dstage = [fe(nc.sbuf_tensor(f"dstage{i}", [128, 512], F32, side="right")) for i in range(2)]

    def em(s):
        for tt in range(8):
            d_out[tt].wait(s, d_out[tt].n)
        c_wc.wait(s)
        c_wce.wait(s)
        d_gsc.wait(s)
        for dt in range(DT):
            for hf in range(2):
                d_xn2.inc(s.dma_start(
                    xn_own[:, dt * 1024 + hf * 512:dt * 1024 + hf * 512 + 512],
                    xn_sp.ap()[4096 + hf * 512:4096 + hf * 512 + 512,
                               dt * 128:(dt + 1) * 128],
                    transpose=True), 16)
    blk.sync(em)

    def em(g):
        d_gsc.wait(g)
    blk.gpsimd(em)

    GU_PS = [(P01, P23), (P45, P67)]
    wg_used = {}
    dwt_used = {}
    sl_used = {}
    dst_used = {}
    silu_done = {}
    m_done = {}
    first_ffn = [True]

    for ic in range(6):
        nit = IC_SIZES[ic]
        it0 = IC_BASE[ic]
        dn_base = c_dn.n
        dne_base = c_dne.n

        groups = []
        q0 = 0
        while q0 < nit:
            n = min(4, nit - q0)
            groups.append((q0, n))
            q0 += n

        for gi, (q0, qn) in enumerate(groups):
            for mat in (0, 1):
                wsl = mat
                src = gsc if mat == 0 else usc

                def em(s, wsl=wsl, src=src, it0=it0, q0=q0, qn=qn):
                    if ("wg", wsl) in wg_used:
                        c_gu.wait(s, wg_used[("wg", wsl)])
                    d_xn2.wait(s)
                    if qn == 4:
                        d_wg[wsl].inc(s.dma_start(
                            wg_sl[wsl][:].rearrange("p (t c) -> p t c", c=512),
                            src.ap()[:, (it0 + q0) * 128:(it0 + q0 + qn) * 128]
                            .rearrange("(t p) c -> p t c", p=128)), 16)
                    else:
                        for dt in range(DT):
                            d_wg[wsl].inc(s.dma_start(
                                wg_sl[wsl][:].rearrange("p (t c) -> p t c", t=DT)
                                [:, dt, 0:qn * 128],
                                src.ap()[dt * 128:(dt + 1) * 128,
                                         (it0 + q0) * 128:
                                         (it0 + q0 + qn) * 128]), 16)
                blk.sync(em)
                if mat == 0:
                    wg_loads_g = d_wg[0].n
                else:
                    wg_loads_u = d_wg[1].n

            for itl in range(qn):
                it = q0 + itl
                par = it % 2
                PG, PU = GU_PS[par]

                def em(t, itl=itl, par=par, PG=PG, wl=wg_loads_g, dneb=dne_base):
                    d_wg[0].wait(t, wl)
                    if first_ffn[0]:
                        c_wce.wait(t)
                        first_ffn[0] = False
                    c_dne.wait(t, dneb)
                    if ("si", par) in silu_done:
                        c_sl.wait(t, silu_done[("si", par)])
                    for hf in range(2):
                        for dt in range(DT):
                            c_gu.inc(t.matmul(
                                PG[:, hf * 512:hf * 512 + 512],
                                wg_sl[0][:, dt * 512 + itl * 128:
                                          dt * 512 + (itl + 1) * 128],
                                xn_own[:, dt * 1024 + hf * 512:
                                       dt * 1024 + hf * 512 + 512],
                                start=(dt == 0), stop=(dt == DT - 1)))
                blk.tensor(em)
                ga = c_gu.n

                def em(sc, par=par, PG=PG, ga=ga, it=it):
                    c_gu.wait(sc, ga)
                    d_aux.wait(sc)
                    ssl = it % 2
                    if ("sl", ssl) in sl_used:
                        c_m.wait(sc, sl_used[("sl", ssl)])
                    c_sl.inc(sc.activation(silu_st[ssl][:], PG[:],
                                           AF.Silu, scale=sb_scal[:, 0:1]))
                blk.scalar(em)
                silu_done[("si", par)] = c_sl.n

                def em(t, itl=itl, par=par, PU=PU, wl=wg_loads_u, dneb=dne_base):
                    d_wg[1].wait(t, wl)
                    c_dne.wait(t, dneb)
                    if ("mu", par) in m_done:
                        c_m.wait(t, m_done[("mu", par)])
                    for hf in range(2):
                        for dt in range(DT):
                            c_gu.inc(t.matmul(
                                PU[:, hf * 512:hf * 512 + 512],
                                wg_sl[1][:, dt * 512 + itl * 128:
                                          dt * 512 + (itl + 1) * 128],
                                xn_own[:, dt * 1024 + hf * 512:
                                       dt * 1024 + hf * 512 + 512],
                                start=(dt == 0), stop=(dt == DT - 1)))
                blk.tensor(em)
                ua = c_gu.n

                def em(v, par=par, PU=PU, ua=ua, it=it, q0=q0, itl=itl,
                       db=dn_base):
                    c_gu.wait(v, ua)
                    c_dn.wait(v, db)
                    ssl = it % 2
                    c_sl.wait(v, silu_done[("si", par)])
                    c_m.inc(v.tensor_tensor(
                        m_sb[:, (q0 + itl) * 1024:(q0 + itl + 1) * 1024],
                        PU[:], silu_st[ssl][:], ALU.mult))
                blk.vector(em)
                sl_used[("sl", it % 2)] = c_m.n
                m_done[("mu", par)] = c_m.n
            wg_used[("wg", 0)] = c_gu.n
            wg_used[("wg", 1)] = c_gu.n
        m_all = c_m.n

        for e8 in range(DEG):
            dsl = e8 % 2

            def em(s, e8=e8, dsl=dsl, nit=nit, it0=it0):
                if ("dw", dsl) in dwt_used:
                    c_dn.wait(s, dwt_used[("dw", dsl)])
                for itl in range(0, nit, 2):
                    d_dw[dsl].inc(s.dma_start(
                        dwt_sl[dsl][:, itl * 512:(itl + 1) * 512],
                        dsc.ap()[(it0 + itl) * 128:(it0 + itl + 1) * 128,
                                 e8 * 512:(e8 + 1) * 512]), 16)
            blk.sync(em)

            def em(sc, e8=e8, dsl=dsl, nit=nit, it0=it0):
                if ("dw", dsl) in dwt_used:
                    c_dn.wait(sc, dwt_used[("dw", dsl)])
                for itl in range(1, nit, 2):
                    d_dw[dsl].inc(sc.dma_start(
                        dwt_sl[dsl][:, itl * 512:(itl + 1) * 512],
                        dsc.ap()[(it0 + itl) * 128:(it0 + itl + 1) * 128,
                                 e8 * 512:(e8 + 1) * 512]), 16)
            blk.scalar(em)
            dw_loads = d_dw[dsl].n

            for tt in range(8):
                bank = tt % 2
                bslice = [P01[:, 0:512], P23[:, 0:512]][bank]

                def em(t, tt=tt, dsl=dsl, nit=nit, bank=bank, bslice=bslice,
                       ma=m_all, dl=dw_loads):
                    d_dw[dsl].wait(t, dl)
                    c_m.wait(t, ma)
                    if ("db", bank) in dst_used:
                        c_dne.wait(t, dst_used[("db", bank)])
                    for itl in range(nit):
                        c_dn.inc(t.matmul(
                            bslice,
                            m_sb[:, itl * 1024 + tt * 128:itl * 1024 + (tt + 1) * 128],
                            dwt_sl[dsl][:, itl * 512:(itl + 1) * 512],
                            start=(itl == 0), stop=(itl == nit - 1)))
                blk.tensor(em)
                da = c_dn.n

                def em(v, bank=bank, bslice=bslice, da=da):
                    c_dn.wait(v, da)
                    c_misc.wait(v, 2)
                    if ("ds", bank) in dst_used:
                        tt2, n2 = dst_used[("ds", bank)]
                        d_outa[tt2].wait(v, n2)
                    c_dne.inc(v.tensor_scalar(
                        dstage[bank][:], bslice, sb_usds[:], None, ALU.mult))
                blk.vector(em)
                dst_used[("db", bank)] = c_dne.n
                dnea = c_dne.n

                def em(g, ic=ic, e8=e8, tt=tt, bank=bank, da=dnea):
                    c_dne.wait(g, da)
                    d_out[tt].wait(g, wc_out_base[tt])
                    d_outa[tt].wait(g, 16 * DEG * ic)
                    d_outa[tt].inc(g.dma_start(
                        out.ap()[tt * 128:(tt + 1) * 128, e8 * 512:(e8 + 1) * 512],
                        dstage[bank][:], accum_op=ALU.add), 16)
                blk.gpsimd(em)
                dst_used[("ds", bank)] = (tt, d_outa[tt].n)
            dwt_used[("dw", dsl)] = c_dn.n

    def em(g):
        if DEBUG_DUMP:
            c_m.wait(g)
            g.dma_start(dbg_m.ap(), m_sb[:]).then_inc(d_wkv.h, 16)
            d_wkv.wait(g, d_wkv.n + 16)
    blk.gpsimd(em)

    ffn_scope.close()

    def em(g):
        for tt in range(8):
            d_outa[tt].wait(g, d_outa[tt].n)
        for sm in [d_gsc, d_wkv] + d_wq + d_as + d_wcq:
            sm.wait(g, sm.n)
        if DEBUG_DUMP:
            g.dma_start(dbg_xn.ap(), xn_sp.ap()).then_inc(d_gsc.h, 16)
            g.dma_start(dbg_at.ap(), attn_sp.ap()).then_inc(d_gsc.h, 16)
            d_gsc.wait(g, d_gsc.n + 32)
    blk.gpsimd(em)

    def em(s):
        for tt in range(8):
            d_out[tt].wait(s, d_out[tt].n)
        for sm in [d_aux, d_at, d_xn2] + d_x + d_xt + d_xo + d_wg + d_dw:
            sm.wait(s, sm.n)
    blk.sync(em)

    def em(sc):
        for sm in d_xnw:
            sm.wait(sc, sm.n)
    blk.scalar(em)

    st.close()
    return nc


_NC_CACHE = []


def kernel(L0_attn_hidden_states, L0_attn_w_qkv, L0_attn_w_c,
           L0_ffn_gate_w, L0_ffn_gate_s, L0_ffn_up_w, L0_ffn_up_s,
           L0_ffn_down_w, L0_ffn_down_s):
    x = np.ascontiguousarray(np.asarray(L0_attn_hidden_states, dtype=np.float32))
    w_qkv = np.ascontiguousarray(np.asarray(L0_attn_w_qkv, dtype=np.float32))
    w_c = np.ascontiguousarray(np.asarray(L0_attn_w_c, dtype=np.float32))

    def i8view(w):
        w = np.ascontiguousarray(np.asarray(w, dtype=np.int32))
        return np.ascontiguousarray(w.view(np.int8)[:, 0::4])

    gw8 = np.ascontiguousarray(i8view(L0_ffn_gate_w).T)
    uw8 = np.ascontiguousarray(i8view(L0_ffn_up_w).T)
    dw8 = np.ascontiguousarray(i8view(L0_ffn_down_w).T)

    scal = np.empty((128, 3), np.float32)
    scal[:, 0] = np.asarray(L0_ffn_gate_s, np.float32).reshape(())
    scal[:, 1] = np.asarray(L0_ffn_up_s, np.float32).reshape(())
    scal[:, 2] = np.asarray(L0_ffn_down_s, np.float32).reshape(())

    ii = np.arange(128, dtype=np.float64)
    r2l = np.empty((7, H * 128), np.float64)
    for h in range(H):
        A = float(SLOPES[h]) / SCALE
        u = A * ii
        u_hi = u.astype(ml_dtypes.bfloat16).astype(np.float64)
        u_lo = u - u_hi
        A_hi = np.float64(np.asarray(A, dtype=ml_dtypes.bfloat16).astype(np.float64))
        A_lo = A - A_hi
        r2l[0, h * 128:(h + 1) * 128] = u_hi
        r2l[1, h * 128:(h + 1) * 128] = u_lo
        r2l[2, h * 128:(h + 1) * 128] = A_hi
        r2l[3, h * 128:(h + 1) * 128] = A_hi
        r2l[4, h * 128:(h + 1) * 128] = A_lo
        r2l[5, h * 128:(h + 1) * 128] = -128.0 * A_hi
        r2l[6, h * 128:(h + 1) * 128] = -128.0 * A_lo
    r2l = np.concatenate([r2l, np.zeros((121, H * 128))]).astype(ml_dtypes.bfloat16)
    jj = np.arange(512, dtype=np.float64)
    jh = 4.0 * np.floor(jj / 4.0)
    jl = jj - jh
    on = np.ones(512)
    zr = np.zeros(512)
    half0 = np.stack([on, on, -jh, -jl, -jh, zr, zr])
    half1 = np.stack([on, on, -jh, -jl, -jh, on, on])
    r2r = np.concatenate([np.concatenate([half0, half1], axis=1),
                          np.zeros((121, 1024))]).astype(ml_dtypes.bfloat16)
    tri = np.where(np.arange(128)[None, :] >= np.arange(128)[:, None],
                   0.0, NEG).astype(np.float32)

    ident = np.eye(128, dtype=ml_dtypes.bfloat16)

    in_maps = []
    for c in range(8):
        b, p = c // 4, c % 4
        b0 = 1024 * p
        xctx = np.zeros((CTX, E), np.float32)
        lo_abs = b0 - 4096
        nreal = b0 + 1024 - max(0, lo_abs)
        xctx[CTX - nreal:] = x[b, max(0, lo_abs):b0 + 1024]
        npad = 32 - 8 * p
        bias_tab = np.zeros((128, H * 80), np.float32)
        for h in range(H):
            for w in range(2):
                for kt in range(40):
                    v = float(SLOPES[h]) * (128.0 * kt - 4096.0 - 512.0 * w)
                    if kt < npad:
                        v = NEG
                    bias_tab[:, h * 80 + w * 40 + kt] = v
        in_maps.append(dict(
            x_ctx=xctx, w_qkv=w_qkv, w_c=w_c, gw8=gw8, uw8=uw8, dw8=dw8,
            scal=scal, aux_r2l=r2l, aux_r2r=r2r,
            aux_bias=bias_tab, aux_tri=tri, aux_id=ident))

    if not _NC_CACHE:
        _NC_CACHE.append(build_nc())
    nc = _NC_CACHE[0]
    res = run_bass_kernel_spmd(nc, in_maps, core_ids=list(range(8)))

    outp = np.empty((B, S, E), np.float32)
    for c in range(8):
        b, p = c // 4, c % 4
        outp[b, 1024 * p:1024 * (p + 1)] = res.results[c]["out"]
    return outp
